# version 07
# baseline (speedup 1.0000x reference)
import numpy as np
import concourse.bass as bass
import concourse.mybir as mybir
from concourse.bass_utils import run_bass_kernel_spmd

F32 = mybir.dt.float32
ALU = mybir.AluOpType
AF = mybir.ActivationFunctionType
AX = mybir.AxisListType

D = 2048
DC = 16
H = 16
HD = 128
SCALE = HD ** -0.5
EPS_ATTN = 1e-6
EPS_LN = 1e-5
EPS_TOPK = 1e-9
IC_FF = 8
NBUF = 3

ENGS = ("sync", "gpsimd", "vector", "scalar", "tensor")
PHASES = ("init", "ln_oth", "state", "ln_own", "attn", "router", "experts")


BLOB_ITEMS = [
    ("ident", (128, 128)),
    ("onec", (128, 1)),
    ("onerow", (1, 128)),
    ("masku", (128, 128)),
    ("masku4", (128, 512)),
    ("sel4", (4, 512)),
    ("ones4", (4, 4)),
    ("mn2", (4, 2048)),
    ("wrT", (128, 256)),
    ("mnT", (128, 16, 4)),
    ("memT", (16, 128, 512)),
    ("wqT", (16, 128, 2048)),
    ("wkT", (16, 128, 2048)),
    ("wvT", (16, 128, 2048)),
    ("woT", (16, 128, 2048)),
    ("swg", (8, 128, 2048)),
    ("swu", (8, 128, 2048)),
    ("swd", (16, 128, 1024)),
    ("wg", (16, 8, 128, 2048)),
    ("wu", (16, 8, 128, 2048)),
    ("wd", (16, 16, 128, 1024)),
]


def blob_offsets():
    offs, o = {}, 0
    for name, shape in BLOB_ITEMS:
        n = 1
        for s in shape:
            n *= s
        offs[name] = (o, shape)
        o += n
    return offs, o


BLOB_OFFS, BLOB_N = blob_offsets()


def boff(name, *idx):
    off, shape = BLOB_OFFS[name]
    rem = 1
    for s in shape[len(idx):]:
        rem *= s
    flat = 0
    for k in range(len(idx)):
        flat = flat * shape[k] + idx[k]
    return off + flat * rem


def _strips(W):
    K, N = W.shape
    Kc, Nc = K // 128, N // 128
    return np.ascontiguousarray(
        W.reshape(Kc, 128, Nc, 128).transpose(2, 1, 0, 3).reshape(Nc, 128, Kc * 128))


def _strips_small(W):
    K, N = W.shape
    Kc = K // 128
    return np.ascontiguousarray(W.reshape(Kc, 128, N).transpose(1, 0, 2).reshape(128, Kc * N))


def build_blob(i):
    f = np.float32
    parts = {}
    parts["ident"] = np.eye(128, dtype=f)
    parts["onec"] = np.ones((128, 1), dtype=f)
    parts["onerow"] = np.ones((1, 128), dtype=f)
    parts["masku"] = np.triu(np.ones((128, 128), dtype=f))
    parts["masku4"] = np.tile(parts["masku"], (1, 4))
    s4 = np.zeros((4, 512), dtype=f)
    for m in range(4):
        s4[m, m * 128:(m + 1) * 128] = 1.0
    parts["sel4"] = s4
    parts["ones4"] = np.ones((4, 4), dtype=f)
    mn = np.asarray(i["L0_attn_memory_norms"], dtype=f)
    parts["mn2"] = np.ascontiguousarray(mn.reshape(4, 2048))
    parts["wrT"] = _strips_small(np.ascontiguousarray(np.asarray(i["L0_ffn_Wr"], f).T))
    parts["mnT"] = np.ascontiguousarray(mn.transpose(2, 1, 0))
    mem = np.asarray(i["L0_attn_memories"], dtype=f)
    parts["memT"] = np.ascontiguousarray(mem.transpose(1, 2, 0, 3).reshape(16, 128, 512))
    for nm, key in (("wqT", "L0_attn_w_q"), ("wkT", "L0_attn_w_k"),
                    ("wvT", "L0_attn_w_v"), ("woT", "L0_attn_w_o")):
        parts[nm] = _strips(np.ascontiguousarray(np.asarray(i[key], f).T))
    parts["swg"] = _strips(np.asarray(i["L0_ffn_sWg"], f))
    parts["swu"] = _strips(np.asarray(i["L0_ffn_sWu"], f))
    parts["swd"] = _strips(np.asarray(i["L0_ffn_sWd"], f))
    parts["wg"] = np.stack([_strips(np.asarray(i["L0_ffn_Wg"], f)[e]) for e in range(16)])
    parts["wu"] = np.stack([_strips(np.asarray(i["L0_ffn_Wu"], f)[e]) for e in range(16)])
    parts["wd"] = np.stack([_strips(np.asarray(i["L0_ffn_Wd"], f)[e]) for e in range(16)])
    blob = np.empty(BLOB_N, dtype=f)
    for name, (off, shape) in BLOB_OFFS.items():
        a = parts[name]
        assert tuple(a.shape) == tuple(shape), (name, a.shape, shape)
        blob[off:off + a.size] = a.ravel()
    return blob



class Ch:
    __slots__ = ("sem", "n", "step")

    def __init__(self, sem, step=1):
        self.sem = sem
        self.n = 0
        self.step = step


class Builder:
    def __init__(self, nc):
        self.nc = nc
        self.spacer = None
        self.ops = {e: [] for e in ENGS}

    @staticmethod
    def _tiny(ap):
        try:
            pat = ap.ap
            parts = pat[0][1]
            free = 1
            for sc in pat[1:]:
                free *= sc[1]
            return parts < 128 or free < 128
        except Exception:
            return True

    def _vspace(self, inc):
        sp = self.spacer
        self.emit("vector", lambda e, o=sp[0], i=sp[1]: e.tensor_copy(o, i))
        self.emit("vector", lambda e, o=sp[0], i=sp[1]: e.tensor_copy(o, i),
                  (), inc)

    def _aspace(self, inc):
        sp = self.aspacer
        self.emit("scalar", lambda e, o=sp[0], i=sp[1]: e.copy(o, i))
        self.emit("scalar", lambda e, o=sp[0], i=sp[1]: e.copy(o, i),
                  (), inc)

    def emit(self, eng, fn, waits=(), inc=None):
        seen = {}
        for ch, v in waits:
            if v is None or v <= 0:
                continue
            assert v <= ch.n, f"wait target {v} > emitted {ch.n} ({eng})"
            if id(ch) in seen and seen[id(ch)][1] >= v:
                continue
            seen[id(ch)] = (ch, v)
        wl = list(seen.values())
        il = None
        if inc is not None:
            ch, k = inc
            ch.n += k
            il = (ch.sem, k)

        def g(E, wl=wl, il=il, fn=fn):
            for ch, v in wl:
                E.wait_ge(ch.sem, v)
            ins = fn(E)
            if il is not None:
                ins.then_inc(il[0], il[1])
        self.ops[eng].append(g)

    def mm(self, out, lhsT, rhs, start, stop, waits=(), inc=None):
        self.emit("tensor",
                  lambda t, o=out, l=lhsT, r=rhs, a=start, bb=stop:
                  t.matmul(o, l, r, start=a, stop=bb), waits, inc)

    def tr(self, out, in_, ident, waits=(), inc=None):
        self.emit("tensor",
                  lambda t, o=out, i=in_, d=ident: t.transpose(o, i, d), waits, inc)

    def tt(self, out, a, b_, op, waits=(), inc=None):
        t = self._tiny(out)
        self.emit("vector",
                  lambda e, o=out, x=a, y=b_, p=op: e.tensor_tensor(o, x, y, p),
                  waits, None if t else inc)
        if t:
            self._vspace(inc)

    def ts(self, out, a, s1, op0, s2=None, op1=None, waits=(), inc=None):
        t = self._tiny(out)
        if op1 is None:
            self.emit("vector", lambda e, o=out, x=a, s=s1, p=op0:
                      e.tensor_scalar(o, x, s, None, p), waits, None if t else inc)
        else:
            self.emit("vector", lambda e, o=out, x=a, sa=s1, sb=s2, p0=op0, p1=op1:
                      e.tensor_scalar(o, x, sa, sb, p0, p1), waits, None if t else inc)
        if t:
            self._vspace(inc)

    def stt(self, out, a, s, b_, op0, op1, waits=(), inc=None):
        t = self._tiny(out)
        self.emit("vector", lambda e, o=out, x=a, sc=s, y=b_, p0=op0, p1=op1:
                  e.scalar_tensor_tensor(o, x, sc, y, p0, p1), waits,
                  None if t else inc)
        if t:
            self._vspace(inc)

    def act(self, out, in_, func, bias=None, scale=None, waits=(), inc=None):
        t = self._tiny(out)

        def f(e, o=out, i=in_, fu=func, bi=bias, s=scale):
            kw = {}
            if bi is not None:
                kw["bias"] = bi
            if s is not None:
                kw["scale"] = s
            return e.activation(o, i, fu, **kw)
        self.emit("scalar", f, waits, None if t else inc)
        if t:
            self._aspace(inc)

    def red(self, out, in_, op, waits=(), inc=None):
        self.emit("vector", lambda e, o=out, i=in_, p=op:
                  e.tensor_reduce(o, i, AX.X, p), waits, None)
        sp = self.spacer
        self.emit("vector", lambda e, o=sp[0], i=sp[1]: e.tensor_copy(o, i))
        self.emit("vector", lambda e, o=sp[0], i=sp[1]: e.tensor_copy(o, i),
                  (), inc)

    def recip(self, out, in_, waits=(), inc=None):
        t = self._tiny(out)
        self.emit("vector", lambda e, o=out, i=in_: e.reciprocal(o, i), waits,
                  None if t else inc)
        if t:
            self._vspace(inc)

    def cp(self, out, in_, waits=(), inc=None):
        t = self._tiny(out)
        self.emit("vector", lambda e, o=out, i=in_: e.tensor_copy(o, i), waits,
                  None if t else inc)
        if t:
            self._vspace(inc)

    def memset(self, ap, val, waits=(), inc=None):
        self.emit("vector", lambda e, a=ap, v=val: e.memset(a, v), waits, inc)

    def dma(self, eng, out, in_, waits=(), inc=None):
        self.emit(eng, lambda e, o=out, i=in_: e.dma_start(out=o, in_=i), waits, inc)




def build_nc(dump=None, upto=None):
    if upto == "attn1":
        run = PHASES[:PHASES.index("ln_own") + 1]
    else:
        run = PHASES if upto is None else PHASES[:PHASES.index(upto) + 1]
    nc = bass.Bass("TRN2", target_bir_lowering=False)
    xT = nc.dram_tensor("xT", [D, 2048], F32, kind="ExternalInput").ap()
    blob = nc.dram_tensor("blob", [BLOB_N], F32, kind="ExternalInput").ap()
    cpage_h = nc.dram_tensor("cpage", [128, 32], F32, kind="ExternalInput").ap()
    outT = nc.dram_tensor("outT", [D, 1024], F32, kind="ExternalOutput").ap()
    dbg = None
    if dump is not None:
        dbg = nc.dram_tensor("dbg", [128, 16384], F32, kind="ExternalOutput").ap()

    blob_t = blob.tensor

    AR_N = 8192
    xn = nc.alloc_sbuf_tensor("xn", [128, DC * 1024], F32).ap()
    acc = nc.alloc_sbuf_tensor("acc", [128, DC * 1024], F32).ap()
    ar = nc.alloc_sbuf_tensor("ar", [128, AR_N], F32).ap()
    ring = nc.alloc_sbuf_tensor("ring", [128, NBUF * 2048], F32).ap()
    S_sb = nc.alloc_sbuf_tensor("S_sb", [128, H * 128], F32).ap()
    Z_sb = nc.alloc_sbuf_tensor("Z_sb", [128, 16], F32).ap()
    mxn = nc.alloc_sbuf_tensor("mxn", [128, 32], F32).ap()
    mnT_sb = nc.alloc_sbuf_tensor("mnT_sb", [128, 64], F32).ap()
    r0 = nc.alloc_sbuf_tensor("r0", [1, 1536], F32).ap()
    sm = nc.alloc_sbuf_tensor("sm", [128, 32], F32).ap()
    wgtg = nc.alloc_sbuf_tensor("wgtg", [4, 16], F32).ap()
    idn = nc.alloc_sbuf_tensor("idn", [128, 128], F32).ap()
    msku4 = nc.alloc_sbuf_tensor("msku4", [128, 512], F32).ap()
    sel4 = nc.alloc_sbuf_tensor("sel4", [4, 512], F32).ap()
    cpg = nc.alloc_sbuf_tensor("cpg", [128, 32], F32).ap()
    sgr = nc.alloc_sbuf_tensor("sgr", [1, 48], F32).ap()
    sgb = nc.alloc_sbuf_tensor("sgb", [128, 16], F32).ap()
    on4 = nc.alloc_sbuf_tensor("on4", [4, 4], F32).ap()
    onr = nc.alloc_sbuf_tensor("onr", [1, 128], F32).ap()

    pb = [nc.alloc_psum_tensor(f"pb{i}", [128, 512], F32).ap() for i in range(8)]

    sems = {n: nc.alloc_semaphore(n) for n in
            ("s_w", "s_x", "s_o", "s_T", "s_V", "s_A")}
    s_w = Ch(sems["s_w"], 16)
    s_x = Ch(sems["s_x"], 16)
    s_o = Ch(sems["s_o"], 16)
    s_T = Ch(sems["s_T"])
    s_V = Ch(sems["s_V"])
    s_A = Ch(sems["s_A"])

    cwT = S_sb[0:16, 1024:2048]
    b = Builder(nc)
    b.spacer = (sm[:, 24:25], sm[:, 31:32])
    b.aspacer = (sm[:, 23:24], sm[:, 31:32])

    def bap(off, P, F):
        return bass.AP(blob_t, off, [[F, P], [1, F]])

    class WS:
        def __init__(self):
            self.cnt = 0
            self.freed = []

        def load(self, off, cols, rows=128):
            i = self.cnt
            slot = i % NBUF
            waits = []
            if i >= NBUF:
                waits.append((s_T, self.freed[i - NBUF]))
            b.dma("sync", ring[0:rows, slot * 2048: slot * 2048 + cols],
                  bap(off, rows, cols), waits=waits, inc=(s_w, 16))
            self.cnt += 1
            return slot, 16 * self.cnt

        def done(self):
            self.freed.append(s_T.n)

    ws = WS()

    class XS:
        def __init__(self):
            self.cnt = 0
            self.freed = []

        def load(self, dst, src, depth=2, waits=()):
            i = self.cnt
            wl = list(waits)
            if i >= depth:
                wl += self.freed[i - depth]
            b.dma("gpsimd", dst, src, waits=wl, inc=(s_x, 16))
            self.cnt += 1

        def done(self, *chs):
            self.freed.append([(c, c.n) for c in chs])

    LN = dict(XS0=0, XS1=1024, SQA=2048, AB=3072, BB=4096,
              R1=5120, R2=6144, R3=7168)
    P2 = dict(KPT=0, KTK=1024, VT=2048, VTK=3072, S1=4096, S2=4608)
    AT = dict(ATTNB=0, SQ=4096, KPT=4608, KTK=5120, VTK=5632,
              S1=6144, S2=6656, SMT=7168, RBM=7296)
    RT = dict(LGT=0, PRB=1024, T0=1040, T1=1056, SCR=1072)
    FF = dict(SU=0)
    R0 = dict(CROW=0, DR=1024)

    one_col = sm[:, 31:32]
    mcol = sm[:, 30:31]
    em4 = sm[:, 28:29]
    tmpc = sm[:, 27:28]
    mq_sb = sm[:, 26:27]
    c1s = sm[:, 25:26]
    m1s = sm[:, 21:22]
    m2s = sm[:, 22:23]
    naec = sm[0:4, 15:16]
    brwc = sm[0:4, 16:17]
    rlc = sm[0:4, 11:12]
    rlc2 = sm[0:4, 12:13]
    scol = sm[0:4, 13:14]
    sc2 = sm[0:4, 14:15]

    def init_phase():
        b.dma("gpsimd", sm[:, 31:32], bap(boff("onec"), 128, 1), inc=(s_x, 16))
        b.dma("gpsimd", onr[0:1, :], bap(boff("onerow"), 1, 128), inc=(s_x, 16))
        b.dma("gpsimd", cpg[:, :], cpage_h[:, :], inc=(s_x, 16))
        b.dma("gpsimd", idn[:, :], bap(boff("ident"), 128, 128), inc=(s_x, 16))
        b.dma("gpsimd", msku4[:, :], bap(boff("masku4"), 128, 512), inc=(s_x, 16))
        b.dma("gpsimd", sel4[:, :], bap(boff("sel4"), 4, 512), inc=(s_x, 16))
        b.dma("gpsimd", on4[:, :], bap(boff("ones4"), 4, 4), inc=(s_x, 16))
        b.dma("gpsimd", mnT_sb[:, :], bap(boff("mnT"), 128, 64), inc=(s_x, 16))
        b.dma("gpsimd", ar[0:4, 0:2048], bap(boff("mn2"), 4, 2048), inc=(s_x, 16))
        b.act(sgr[0:1, 0:16], cpg[0:1, 2:18], AF.Sigmoid,
              waits=[(s_x, 48)], inc=(s_A, 1))
        b.act(sgr[0:1, 16:32], sgr[0:1, 0:16], AF.Copy, bias=1.0, scale=-1.0,
              waits=[(s_A, s_A.n)], inc=(s_A, 1))
        b.mm(pb[4][:, 0:2], onr[0:1, :], cpg[0:1, 20:22], True, True,
             waits=[(s_x, s_x.n), (s_V, s_V.n)], inc=(s_T, 1))
        b.cp(mcol, pb[4][:, 0:1], waits=[(s_T, s_T.n)], inc=(s_V, 1))
        b.mm(pb[5][:, 0:16], onr[0:1, :], sgr[0:1, 0:16], True, True,
             waits=[(s_A, s_A.n), (s_V, s_V.n)], inc=(s_T, 1))
        b.cp(sgb[:, :], pb[5][:, 0:16], waits=[(s_T, s_T.n)], inc=(s_V, 1))
        b.red(em4[0:4, 0:1], ar[0:4, 0:2048], ALU.add,
              waits=[(s_x, s_x.n)], inc=(s_V, 1))
        b.ts(em4[0:4, 0:1], em4[0:4, 0:1], 1e-6, ALU.is_lt,
             waits=[(s_V, s_V.n)], inc=(s_V, 1))
        b.mm(pb[3][0:4, 0:2], on4[0:4, 0:4], sm[0:4, 28:30], True, True,
             waits=[(s_V, s_V.n), (s_x, s_x.n)], inc=(s_T, 1))
        b.ts(naec, pb[3][0:4, 0:1], 3.5, ALU.is_gt,
             waits=[(s_T, s_T.n)], inc=(s_V, 1))
        b.ts(naec, naec, -1.0, ALU.mult, inc=(s_V, 1))
        b.ts(naec, naec, 1.0, ALU.add, inc=(s_V, 1))
        b.ts(brwc, em4[0:4, 0:1], -1e30, ALU.mult, inc=(s_V, 1))

    def ln_phase(col0, own):
        xs = XS()
        for dc in range(DC):
            slot = LN["XS0"] if dc % 2 == 0 else LN["XS1"]
            xt = ar[:, slot:slot + 1024]
            wl = [(s_V, s_V.n), (s_T, s_T.n)] if dc < 2 else ()
            xs.load(xt, bass.AP(xT.tensor, dc * 128 * 2048 + col0,
                                [[2048, 128], [1, 1024]]), waits=wl)
            sq = ar[:, LN["SQA"]:LN["SQA"] + 1024]
            b.act(sq, xt, AF.Square, waits=[(s_x, s_x.n), (s_T, s_T.n)],
                  inc=(s_A, 1))
            for g in range(2):
                b.mm(pb[g][0:1, :], one_col, xt[:, g * 512:(g + 1) * 512],
                     dc == 0, dc == DC - 1,
                     waits=[(s_x, s_x.n)] if g == 0 else (),
                     inc=(s_T, 1) if g == 1 else None)
            for g in range(2):
                b.mm(pb[2 + g][0:1, :], one_col, sq[:, g * 512:(g + 1) * 512],
                     dc == 0, dc == DC - 1,
                     waits=[(s_A, s_A.n)] if g == 0 else (),
                     inc=(s_T, 1) if g == 1 else None)
            xs.done(s_T)
        R1 = ar[0:1, LN["R1"]:LN["R1"] + 1024]
        R2 = ar[0:1, LN["R2"]:LN["R2"] + 1024]
        R3 = ar[0:1, LN["R3"]:LN["R3"] + 1024]
        for g in range(2):
            b.ts(R1[0:1, g * 512:(g + 1) * 512], pb[g][0:1, :], 1.0 / D,
                 ALU.mult, waits=[(s_T, s_T.n)], inc=(s_V, 1))
            b.ts(R2[0:1, g * 512:(g + 1) * 512], pb[2 + g][0:1, :], 1.0 / D,
                 ALU.mult, inc=(s_V, 1))
        b.tt(R3, R1, R1, ALU.mult, inc=(s_V, 1))
        b.tt(R2, R2, R3, ALU.subtract, inc=(s_V, 1))
        b.ts(R2, R2, EPS_LN, ALU.add, inc=(s_V, 1))
        b.recip(R3, R2, inc=(s_V, 1))
        b.act(R2, R3, AF.Sqrt, waits=[(s_V, s_V.n)], inc=(s_A, 1))
        b.stt(R3, R1, -1.0, R2, ALU.mult, ALU.mult,
              waits=[(s_A, s_A.n)], inc=(s_V, 1))
        for half in range(2):
            b.mm(pb[4 + half][:, :], onr[0:1, :],
                 R2[0:1, half * 512:(half + 1) * 512], True, True,
                 waits=[(s_A, s_A.n), (s_V, s_V.n)], inc=(s_T, 1))
            b.cp(ar[:, LN["AB"] + half * 512:LN["AB"] + (half + 1) * 512],
                 pb[4 + half][:, :], waits=[(s_T, s_T.n)], inc=(s_V, 1))
            b.mm(pb[6 + half][:, :], onr[0:1, :],
                 R3[0:1, half * 512:(half + 1) * 512], True, True,
                 waits=[(s_V, s_V.n)], inc=(s_T, 1))
            b.cp(ar[:, LN["BB"] + half * 512:LN["BB"] + (half + 1) * 512],
                 pb[6 + half][:, :], waits=[(s_T, s_T.n)], inc=(s_V, 1))
        a_b = ar[:, LN["AB"]:LN["AB"] + 1024]
        b_b = ar[:, LN["BB"]:LN["BB"] + 1024]
        xs2 = XS()
        for dc in range(DC):
            if own:
                dst = acc[:, dc * 1024:(dc + 1) * 1024]
            else:
                base = LN["XS0"] if dc % 2 == 0 else LN["XS1"]
                dst = ar[:, base:base + 1024]
            wl2 = [(s_T, s_T.n), (s_A, s_A.n)] if (dc < 2 and not own) else ()
            xs2.load(dst, bass.AP(xT.tensor, dc * 128 * 2048 + col0,
                                  [[2048, 128], [1, 1024]]), waits=wl2)
            xnt = xn[:, dc * 1024:(dc + 1) * 1024]
            b.tt(xnt, dst, a_b, ALU.mult, waits=[(s_x, s_x.n)], inc=(s_V, 1))
            b.tt(xnt, xnt, b_b, ALU.add, inc=(s_V, 1))
            b.red(tmpc, xnt, ALU.add, inc=(s_V, 1))
            if own:
                b.tt(mxn[:, dc:dc + 1], mxn[:, dc:dc + 1], tmpc, ALU.add,
                     inc=(s_V, 1))
            else:
                b.cp(mxn[:, dc:dc + 1], tmpc, inc=(s_V, 1))
            xs2.done(s_V)

    def sigma(dst, src_psum, s1c, s2c, n):
        t1 = ar[:, s1c:s1c + n]
        t2 = ar[:, s2c:s2c + n]
        b.ts(t1, src_psum, 0.0, ALU.min, waits=[(s_T, s_T.n)], inc=(s_V, 1))
        b.act(t2, t1, AF.Exp, waits=[(s_V, s_V.n)], inc=(s_A, 1))
        b.stt(dst, src_psum, 0.0, t2, ALU.max, ALU.add,
              waits=[(s_A, s_A.n)], inc=(s_V, 1))

    def state_build():
        for h in range(H):
            slot, wv = ws.load(boff("wkT", h), 2048)
            base = slot * 2048
            for tg in range(2):
                for dc in range(DC):
                    b.mm(pb[4][:, :], ring[:, base + dc * 128:base + (dc + 1) * 128],
                         xn[:, dc * 1024 + tg * 512: dc * 1024 + (tg + 1) * 512],
                         dc == 0, dc == DC - 1,
                         waits=[(s_w, wv), (s_V, s_V.n)] if dc == 0 else (),
                         inc=(s_T, 1) if dc == DC - 1 else None)
                sigma(ar[:, P2["KPT"] + tg * 512: P2["KPT"] + (tg + 1) * 512],
                      pb[4][:, :], P2["S1"], P2["S2"], 512)
            ws.done()
            kpT = ar[:, P2["KPT"]:P2["KPT"] + 1024]
            b.red(tmpc, kpT, ALU.add, inc=(s_V, 1))
            b.ts(Z_sb[:, h:h + 1], tmpc, mcol, ALU.mult, inc=(s_V, 1))
            for q in range(2):
                for c in range(4):
                    cc = q * 4 + c
                    b.tr(pb[6][:, c * 128:(c + 1) * 128],
                         kpT[:, cc * 128:(cc + 1) * 128], idn[:, :],
                         waits=[(s_V, s_V.n)] if c == 0 else (),
                         inc=(s_T, 1) if c == 3 else None)
                for c in range(4):
                    cc = q * 4 + c
                    b.cp(ar[:, P2["KTK"] + cc * 128:P2["KTK"] + (cc + 1) * 128],
                         pb[6][:, c * 128:(c + 1) * 128],
                         waits=[(s_T, s_T.n)] if c == 0 else (), inc=(s_V, 1))
            slot, wv2 = ws.load(boff("wvT", h), 2048)
            base = slot * 2048
            for tg in range(2):
                for dc in range(DC):
                    b.mm(pb[5][:, :], ring[:, base + dc * 128:base + (dc + 1) * 128],
                         xn[:, dc * 1024 + tg * 512: dc * 1024 + (tg + 1) * 512],
                         dc == 0, dc == DC - 1,
                         waits=[(s_w, wv2), (s_V, s_V.n)] if dc == 0 else (),
                         inc=(s_T, 1) if dc == DC - 1 else None)
                b.cp(ar[:, P2["VT"] + tg * 512:P2["VT"] + (tg + 1) * 512],
                     pb[5][:, :], waits=[(s_T, s_T.n)], inc=(s_V, 1))
            ws.done()
            vT = ar[:, P2["VT"]:P2["VT"] + 1024]
            for q in range(2):
                for c in range(4):
                    cc = q * 4 + c
                    b.tr(pb[6][:, c * 128:(c + 1) * 128],
                         vT[:, cc * 128:(cc + 1) * 128], idn[:, :],
                         waits=[(s_V, s_V.n)] if c == 0 else (),
                         inc=(s_T, 1) if c == 3 else None)
                for c in range(4):
                    cc = q * 4 + c
                    b.cp(ar[:, P2["VTK"] + cc * 128:P2["VTK"] + (cc + 1) * 128],
                         pb[6][:, c * 128:(c + 1) * 128],
                         waits=[(s_T, s_T.n)] if c == 0 else (), inc=(s_V, 1))
            for c in range(8):
                b.mm(pb[7][:, 0:128],
                     ar[:, P2["KTK"] + c * 128:P2["KTK"] + (c + 1) * 128],
                     ar[:, P2["VTK"] + c * 128:P2["VTK"] + (c + 1) * 128],
                     c == 0, c == 7,
                     waits=[(s_V, s_V.n)] if c == 0 else (),
                     inc=(s_T, 1) if c == 7 else None)
            b.ts(S_sb[:, h * 128:(h + 1) * 128], pb[7][:, 0:128], mcol,
                 ALU.mult, waits=[(s_T, s_T.n)], inc=(s_V, 1))

    def attention(nth=2, nhp=2):
        for th in range(nth):
            tb = 512 * th
            for hp in range(nhp):
                for hh in range(8):
                    h = hp * 8 + hh
                    sq_ap = ar[:, AT["SQ"]:AT["SQ"] + 512]
                    kpT = ar[:, AT["KPT"]:AT["KPT"] + 512]
                    ktk = ar[:, AT["KTK"]:AT["KTK"] + 512]
                    vtk = ar[:, AT["VTK"]:AT["VTK"] + 512]
                    slot, wv = ws.load(boff("wqT", h), 2048)
                    base = slot * 2048
                    for dc in range(DC):
                        b.mm(pb[0][:, :], ring[:, base + dc * 128:base + (dc + 1) * 128],
                             xn[:, dc * 1024 + tb: dc * 1024 + tb + 512],
                             dc == 0, dc == DC - 1,
                             waits=[(s_w, wv), (s_V, s_V.n)] if dc == 0 else (),
                             inc=(s_T, 1) if dc == DC - 1 else None)
                    if th == 0:
                        for dc in range(DC):
                            b.mm(pb[3][:, 0:2],
                                 ring[:, base + dc * 128:base + (dc + 1) * 128],
                                 mxn[:, dc:dc + 2], dc == 0, dc == DC - 1,
                                 inc=(s_T, 1) if dc == DC - 1 else None)
                    sigma(sq_ap, pb[0][:, :], AT["S1"], AT["S2"], 512)
                    ws.done()
                    if th == 0:
                        b.cp(mq_sb, pb[3][:, 0:1], waits=[(s_T, s_T.n)],
                             inc=(s_V, 1))
                        b.mm(pb[3][0:4, 4:6], mnT_sb[:, h * 4:(h + 1) * 4],
                             sm[:, 26:28], True, True, waits=[(s_V, s_V.n)],
                             inc=(s_T, 1))
                        b.ts(rlc, pb[3][0:4, 4:5], SCALE / 2048.0, ALU.mult,
                             waits=[(s_T, s_T.n)], inc=(s_V, 1))
                        b.tt(rlc, rlc, brwc, ALU.add, inc=(s_V, 1))
                        b.act(rlc2, rlc, AF.Exp, waits=[(s_V, s_V.n)],
                              inc=(s_A, 1))
                        b.mm(pb[3][0:4, 6:8], on4[0:4, 0:4], sm[0:4, 12:14],
                             True, True, waits=[(s_A, s_A.n)], inc=(s_T, 1))
                        b.ts(scol, pb[3][0:4, 6:7], 1e-30, ALU.add,
                             waits=[(s_T, s_T.n)], inc=(s_V, 1))
                        b.recip(sc2, scol, inc=(s_V, 1))
                        b.tt(wgtg[0:4, h:h + 1], rlc2, sc2, ALU.mult,
                             inc=(s_V, 1))
                        b.tt(wgtg[0:4, h:h + 1], wgtg[0:4, h:h + 1],
                             sgb[0:4, h:h + 1], ALU.mult, inc=(s_V, 1))
                        b.tt(wgtg[0:4, h:h + 1], wgtg[0:4, h:h + 1],
                             naec, ALU.mult, inc=(s_V, 1))
                    slot, wv = ws.load(boff("wkT", h), 2048)
                    base = slot * 2048
                    for dc in range(DC):
                        b.mm(pb[0][:, :], ring[:, base + dc * 128:base + (dc + 1) * 128],
                             xn[:, dc * 1024 + tb: dc * 1024 + tb + 512],
                             dc == 0, dc == DC - 1,
                             waits=[(s_w, wv), (s_V, s_V.n)] if dc == 0 else (),
                             inc=(s_T, 1) if dc == DC - 1 else None)
                    sigma(kpT, pb[0][:, :], AT["S1"], AT["S2"], 512)
                    ws.done()
                    for c in range(4):
                        b.tr(pb[4][:, c * 128:(c + 1) * 128],
                             kpT[:, c * 128:(c + 1) * 128], idn[:, :],
                             waits=[(s_V, s_V.n)] if c == 0 else (),
                             inc=(s_T, 1) if c == 3 else None)
                    for c in range(4):
                        b.cp(ktk[:, c * 128:(c + 1) * 128],
                             pb[4][:, c * 128:(c + 1) * 128],
                             waits=[(s_T, s_T.n)] if c == 0 else (), inc=(s_V, 1))
                    slot, wv = ws.load(boff("wvT", h), 2048)
                    base = slot * 2048
                    for c in range(4):
                        bank = pb[1] if c % 2 == 0 else pb[2]
                        for dc in range(DC):
                            b.mm(bank[:, 0:128],
                                 xn[:, dc * 1024 + tb + c * 128:
                                    dc * 1024 + tb + (c + 1) * 128],
                                 ring[:, base + dc * 128:base + (dc + 1) * 128],
                                 dc == 0, dc == DC - 1,
                                 waits=[(s_w, wv), (s_V, s_V.n)] if dc == 0 else (),
                                 inc=(s_T, 1) if dc == DC - 1 else None)
                        b.cp(vtk[:, c * 128:(c + 1) * 128], bank[:, 0:128],
                             waits=[(s_T, s_T.n)], inc=(s_V, 1))
                    ws.done()
                    SMT5 = AT["SMT"]
                    SV = AT["SMT"] + 512
                    zb = sm[:, 4:8]
                    zc = sm[:, 17:21]
                    for c in range(4):
                        b.mm(pb[6][:, c * 128:(c + 1) * 128],
                             ktk[:, c * 128:(c + 1) * 128],
                             vtk[:, c * 128:(c + 1) * 128], True, True,
                             waits=[(s_V, s_V.n)] if c == 0 else (),
                             inc=(s_T, 1) if c == 3 else None)
                    for c in range(4):
                        b.mm(pb[4][:, c * 128:(c + 1) * 128],
                             kpT[:, c * 128:(c + 1) * 128],
                             sq_ap[:, c * 128:(c + 1) * 128], True, True,
                             inc=(s_T, 1) if c == 3 else None)
                    smt5 = ar[:, SMT5:SMT5 + 512]
                    b.tt(smt5, pb[4][:, 0:512], msku4[:, :], ALU.mult,
                         waits=[(s_T, s_T.n)], inc=(s_V, 1))
                    kp3d = bass.AP(ar.tensor, AT["KPT"],
                                   [[AR_N, 128], [128, 4], [1, 128]])
                    b.red(zb, kp3d, ALU.add, inc=(s_V, 1))
                    b.cp(zc[:, 0:1], Z_sb[:, h:h + 1], inc=(s_V, 1))
                    for c in range(3):
                        b.tt(zc[:, c + 1:c + 2], zc[:, c:c + 1], zb[:, c:c + 1],
                             ALU.add, inc=(s_V, 1))
                    b.tt(Z_sb[:, h:h + 1], zc[:, 3:4], zb[:, 3:4], ALU.add,
                         inc=(s_V, 1))
                    sv = [S_sb[:, h * 128:(h + 1) * 128]]
                    for c in range(3):
                        dst = ar[:, SV + c * 128:SV + (c + 1) * 128]
                        b.tt(dst, sv[-1], pb[6][:, c * 128:(c + 1) * 128],
                             ALU.add, inc=(s_V, 1))
                        sv.append(dst)
                    sv4 = ar[:, SV + 384:SV + 512]
                    b.tt(sv4, sv[3], pb[6][:, 384:512], ALU.add, inc=(s_V, 1))
                    slot, wv = ws.load(boff("memT", h), 512)
                    base = slot * 2048
                    b.mm(pb[5][0:4, :], mnT_sb[:, h * 4:(h + 1) * 4], sq_ap,
                         True, True, waits=[(s_V, s_V.n)], inc=(s_T, 1))
                    coef = ar[0:4, AT["S1"]:AT["S1"] + 512]
                    b.ts(coef, pb[5][0:4, :], EPS_ATTN, ALU.max,
                         waits=[(s_T, s_T.n)], inc=(s_V, 1))
                    b.recip(coef, coef, inc=(s_V, 1))
                    b.ts(coef, coef, wgtg[0:4, h:h + 1], ALU.mult, inc=(s_V, 1))
                    for m in range(4):
                        b.mm(pb[6][:, :], sel4[0:4, m * 128:(m + 1) * 128],
                             coef, True, True,
                             waits=[(s_V, s_V.n)], inc=(s_T, 1))
                        sqs = ar[:, AT["S2"]:AT["S2"] + 512]
                        b.tt(sqs, sq_ap, pb[6][:, :], ALU.mult,
                             waits=[(s_T, s_T.n)], inc=(s_V, 1))
                        b.mm(pb[5][:, :], ring[:, base + m * 128:base + (m + 1) * 128],
                             sqs, m == 0, m == 3,
                             waits=[(s_w, wv), (s_V, s_V.n)], inc=(s_T, 1))
                    ws.done()
                    vready = s_V.n
                    b.mm(pb[7][0:1, 0:512], one_col, smt5, True, False,
                         waits=[(s_V, vready)])
                    for c in range(4):
                        b.mm(pb[7][0:1, c * 128:(c + 1) * 128], zc[:, c:c + 1],
                             sq_ap[:, c * 128:(c + 1) * 128], False, c == 3,
                             inc=(s_T, 1) if c == 3 else None)
                    nbk = [pb[0], pb[1], pb[2], pb[3]]
                    for c in range(4):
                        b.mm(nbk[c][:, 0:128], vtk[:, c * 128:(c + 1) * 128],
                             smt5[:, c * 128:(c + 1) * 128], True, False)
                        b.mm(nbk[c][:, 0:128], sv[c],
                             sq_ap[:, c * 128:(c + 1) * 128], False, True,
                             inc=(s_T, 1))
                    b.cp(S_sb[:, h * 128:(h + 1) * 128], sv4,
                         waits=[(s_T, s_T.n)], inc=(s_V, 1))
                    dr5 = r0[0:1, R0["DR"]:R0["DR"] + 512]
                    b.ts(dr5, pb[7][0:1, 0:512], EPS_ATTN, ALU.max,
                         waits=[(s_T, s_T.n)], inc=(s_V, 1))
                    b.recip(dr5, dr5, inc=(s_V, 1))
                    b.ts(dr5, dr5, sgr[0:1, 16 + h:17 + h], ALU.mult,
                         inc=(s_V, 1))
                    b.mm(pb[7][:, 0:512], onr[0:1, :], dr5, True, True,
                         waits=[(s_V, s_V.n)], inc=(s_T, 1))
                    rbm5 = ar[:, SMT5:SMT5 + 512]
                    b.cp(rbm5, pb[7][:, 0:512], waits=[(s_T, s_T.n)],
                         inc=(s_V, 1))
                    for c in range(4):
                        ab = ar[:, AT["ATTNB"] + hh * 512 + c * 128:
                                AT["ATTNB"] + hh * 512 + (c + 1) * 128]
                        b.tt(ab, nbk[c][:, 0:128],
                             rbm5[:, c * 128:(c + 1) * 128], ALU.mult,
                             inc=(s_V, 1))
                    abh = ar[:, AT["ATTNB"] + hh * 512:AT["ATTNB"] + (hh + 1) * 512]
                    b.tt(abh, abh, pb[5][:, :], ALU.add, waits=[(s_T, s_T.n)],
                         inc=(s_V, 1))
                for ec in range(DC):
                    slot, wv = ws.load(boff("woT", ec), 2048)
                    base = slot * 2048
                    bank = pb[0] if ec % 2 == 0 else pb[1]
                    for hh in range(8):
                        hc = hp * 8 + hh
                        b.mm(bank[:, :], ring[:, base + hc * 128:base + (hc + 1) * 128],
                             ar[:, AT["ATTNB"] + hh * 512:AT["ATTNB"] + (hh + 1) * 512],
                             hh == 0, hh == 7,
                             waits=[(s_w, wv), (s_V, s_V.n)] if hh == 0 else (),
                             inc=(s_T, 1) if hh == 7 else None)
                    ws.done()
                    at = acc[:, ec * 1024 + tb: ec * 1024 + tb + 512]
                    b.tt(at, at, bank[:, :], ALU.add, waits=[(s_T, s_T.n)],
                         inc=(s_V, 1))

    def router():
        slot, wv = ws.load(boff("wrT"), 256)
        base = slot * 2048
        for tg in range(2):
            for dc in range(DC):
                b.mm(pb[0][0:16, :], ring[:, base + dc * 16:base + (dc + 1) * 16],
                     xn[:, dc * 1024 + tg * 512: dc * 1024 + (tg + 1) * 512],
                     dc == 0, dc == DC - 1,
                     waits=[(s_w, wv), (s_V, s_V.n)] if dc == 0 else (),
                     inc=(s_T, 1) if dc == DC - 1 else None)
            b.ts(ar[0:16, RT["LGT"] + tg * 512:RT["LGT"] + (tg + 1) * 512],
                 pb[0][0:16, :], cpg[0:16, 0:1], ALU.add,
                 waits=[(s_T, s_T.n)], inc=(s_V, 1))
        ws.done()
        for i in range(8):
            lg = ar[0:16, RT["LGT"] + i * 128:RT["LGT"] + (i + 1) * 128]
            b.tr(pb[4][:, 0:16], lg, idn[0:16, 0:16], waits=[(s_V, s_V.n)],
                 inc=(s_T, 1))
            P = ar[:, RT["PRB"]:RT["PRB"] + 16]
            t0 = ar[:, RT["T0"]:RT["T0"] + 16]
            t1 = ar[:, RT["T1"]:RT["T1"] + 16]
            b.cp(P, pb[4][:, 0:16], waits=[(s_T, s_T.n)], inc=(s_V, 1))
            b.red(c1s, P, ALU.max, inc=(s_V, 1))
            b.ts(c1s, c1s, -1.0, ALU.mult, inc=(s_V, 1))
            b.act(P, P, AF.Exp, bias=c1s, waits=[(s_V, s_V.n)], inc=(s_A, 1))
            b.red(c1s, P, ALU.add, waits=[(s_A, s_A.n)], inc=(s_V, 1))
            b.recip(c1s, c1s, inc=(s_V, 1))
            b.ts(P, P, c1s, ALU.mult, inc=(s_V, 1))

            def first_mask(src, mcol_, out_mask):
                b.ts(t0, src, mcol_, ALU.is_equal, inc=(s_V, 1))
                cur, oth = t0, t1
                for k in (1, 2, 4, 8):
                    b.cp(oth[:, 0:k], cur[:, 0:k], inc=(s_V, 1))
                    b.tt(oth[:, k:16], cur[:, k:16], cur[:, 0:16 - k], ALU.add,
                         inc=(s_V, 1))
                    cur, oth = oth, cur
                b.ts(oth, src, mcol_, ALU.is_equal, inc=(s_V, 1))
                b.tt(cur, cur, oth, ALU.subtract, inc=(s_V, 1))
                b.ts(cur, cur, 0.0, ALU.is_equal, inc=(s_V, 1))
                b.tt(out_mask, cur, oth, ALU.mult, inc=(s_V, 1))

            fm1 = ar[:, RT["SCR"]:RT["SCR"] + 16]
            fm2 = ar[:, RT["SCR"] + 16:RT["SCR"] + 32]
            p2_ = ar[:, RT["SCR"] + 32:RT["SCR"] + 48]
            b.red(m1s, P, ALU.max, inc=(s_V, 1))
            first_mask(P, m1s, fm1)
            b.stt(p2_, fm1, -1e30, P, ALU.mult, ALU.add, inc=(s_V, 1))
            b.red(m2s, p2_, ALU.max, inc=(s_V, 1))
            first_mask(p2_, m2s, fm2)
            b.tt(fm1, fm1, fm2, ALU.add, inc=(s_V, 1))
            b.tt(m1s, m1s, m2s, ALU.add, inc=(s_V, 1))
            b.ts(m1s, m1s, EPS_TOPK, ALU.add, inc=(s_V, 1))
            b.recip(m1s, m1s, inc=(s_V, 1))
            b.tt(fm1, fm1, P, ALU.mult, inc=(s_V, 1))
            b.ts(fm1, fm1, m1s, ALU.mult, inc=(s_V, 1))
            b.tr(pb[4][0:16, 16:144], fm1, idn[:, :], waits=[(s_V, s_V.n)],
                 inc=(s_T, 1))
            b.cp(cwT[0:16, i * 128:(i + 1) * 128], pb[4][0:16, 16:144],
                 waits=[(s_T, s_T.n)], inc=(s_V, 1))

    def experts():
        bfree = {i: [(s_V, s_V.n), (s_A, s_A.n)] for i in range(8)}
        for e in range(-1, 16):
            routed = e >= 0
            if routed:
                crow = r0[0:1, R0["CROW"]:R0["CROW"] + 1024]
                b.dma("gpsimd", crow, cwT[e:e + 1, 0:1024],
                      waits=[(s_V, s_V.n)], inc=(s_x, 16))
                cxv = s_x.n
                for tg in range(2):
                    b.mm(pb[6 + tg][:, :], onr[0:1, :],
                         crow[0:1, tg * 512:(tg + 1) * 512], True, True,
                         waits=[(s_x, cxv)] + bfree[6 + tg], inc=(s_T, 1))
                    b.cp(S_sb[:, tg * 512:(tg + 1) * 512], pb[6 + tg][:, :],
                         waits=[(s_T, s_T.n)], inc=(s_V, 1))
                    bfree[6 + tg] = [(s_V, s_V.n)]
            for ic in range(IC_FF):
                if routed:
                    sg_, wvg = ws.load(boff("wg", e, ic), 2048)
                    su_, wvu = ws.load(boff("wu", e, ic), 2048)
                else:
                    sg_, wvg = ws.load(boff("swg", ic), 2048)
                    su_, wvu = ws.load(boff("swu", ic), 2048)
                bg, bu = sg_ * 2048, su_ * 2048
                gb = 0 if ic % 2 == 0 else 4
                ub = 2 if ic % 2 == 0 else 6
                for tg in range(2):
                    for dc in range(DC):
                        b.mm(pb[gb + tg][:, :], ring[:, bg + dc * 128:bg + (dc + 1) * 128],
                             xn[:, dc * 1024 + tg * 512: dc * 1024 + (tg + 1) * 512],
                             dc == 0, dc == DC - 1,
                             waits=([(s_w, wvg)] + bfree[gb + tg]) if dc == 0 else (),
                             inc=(s_T, 1) if dc == DC - 1 else None)
                gv = s_T.n
                for tg in range(2):
                    for dc in range(DC):
                        b.mm(pb[ub + tg][:, :], ring[:, bu + dc * 128:bu + (dc + 1) * 128],
                             xn[:, dc * 1024 + tg * 512: dc * 1024 + (tg + 1) * 512],
                             dc == 0, dc == DC - 1,
                             waits=([(s_w, wvu)] + bfree[ub + tg]) if dc == 0 else (),
                             inc=(s_T, 1) if dc == DC - 1 else None)
                uv = s_T.n
                for tg in range(2):
                    sut = ar[:, FF["SU"] + ic * 1024 + tg * 512:
                             FF["SU"] + ic * 1024 + (tg + 1) * 512]
                    b.act(sut, pb[gb + tg][:, :], AF.Silu, waits=[(s_T, gv)],
                          inc=(s_A, 1))
                    bfree[gb + tg] = [(s_A, s_A.n)]
                    b.tt(sut, sut, pb[ub + tg][:, :], ALU.mult,
                         waits=[(s_A, s_A.n), (s_T, uv)], inc=(s_V, 1))
                    bfree[ub + tg] = [(s_V, s_V.n)]
                    if routed:
                        b.tt(sut, sut, S_sb[:, tg * 512:(tg + 1) * 512],
                             ALU.mult, inc=(s_V, 1))
                ws.done()
                ws.done()
            for hc in range(DC):
                if routed:
                    sd, wvd = ws.load(boff("wd", e, hc), 1024)
                else:
                    sd, wvd = ws.load(boff("swd", hc), 1024)
                bd = sd * 2048
                for tg in range(2):
                    bi = (hc * 2 + tg) % 4
                    bank = pb[bi]
                    for ic in range(IC_FF):
                        b.mm(bank[:, :], ring[:, bd + ic * 128:bd + (ic + 1) * 128],
                             ar[:, FF["SU"] + ic * 1024 + tg * 512:
                                FF["SU"] + ic * 1024 + (tg + 1) * 512],
                             ic == 0, ic == IC_FF - 1,
                             waits=([(s_w, wvd)] + bfree[bi]) if ic == 0 else (),
                             inc=(s_T, 1) if ic == IC_FF - 1 else None)
                    at = acc[:, hc * 1024 + tg * 512: hc * 1024 + (tg + 1) * 512]
                    b.tt(at, at, bank[:, :], ALU.add, waits=[(s_T, s_T.n)],
                         inc=(s_V, 1))
                    bfree[bi] = [(s_V, s_V.n)]
                ws.done()

    init_phase()
    if "ln_oth" in run:
        ln_phase(0, own=False)
    if "state" in run:
        state_build()
    if "ln_own" in run:
        ln_phase(1024, own=True)
    if upto == "attn1":
        attention(nth=1, nhp=1)
    elif "attn" in run:
        attention()
    if "router" in run:
        router()
    if "experts" in run:
        experts()

    def dump_all(ap, cols):
        b.dma("gpsimd", dbg[:, 0:cols], ap,
              waits=[(s_V, s_V.n), (s_A, s_A.n), (s_T, s_T.n)], inc=(s_o, 16))

    if dump == "xn":
        dump_all(xn[:, :], 16384)
    elif dump == "acc":
        dump_all(acc[:, :], 16384)
    elif dump == "state":
        dump_all(S_sb[:, :], 2048)
        b.dma("gpsimd", dbg[:, 2048:2064], Z_sb[:, :], inc=(s_o, 16))
        b.dma("gpsimd", dbg[:, 2064:2080], mxn[:, :], inc=(s_o, 16))
        b.dma("gpsimd", dbg[:, 2080:2112], sm[:, :], inc=(s_o, 16))
    elif dump == "cw":
        b.dma("gpsimd", dbg[0:16, 0:1024], cwT[:, :],
              waits=[(s_V, s_V.n), (s_T, s_T.n)], inc=(s_o, 16))
        b.dma("gpsimd", dbg[0:4, 2048:2064], wgtg[:, :], inc=(s_o, 16))
        b.dma("gpsimd", dbg[0:1, 2064:2112], sgr[:, :], inc=(s_o, 16))
        b.dma("gpsimd", dbg[:, 2112:2144], sm[:, :], inc=(s_o, 16))
    elif dump == "arena":
        dump_all(ar[:, :], AR_N)
        b.dma("gpsimd", dbg[0:4, 8192:8208], wgtg[:, :], inc=(s_o, 16))
        b.dma("gpsimd", dbg[0:1, 8208:8256], sgr[:, :], inc=(s_o, 16))
        b.dma("gpsimd", dbg[:, 8256:8272], Z_sb[:, :], inc=(s_o, 16))
        b.dma("gpsimd", dbg[:, 8272:8304], sm[:, :], inc=(s_o, 16))
        b.dma("gpsimd", dbg[:, 8304:10352], S_sb[:, :], inc=(s_o, 16))
        b.dma("gpsimd", dbg[:, 10352:10384], mxn[:, :], inc=(s_o, 16))

    if upto is None:
        for dc in range(DC):
            b.dma("gpsimd", outT[dc * 128:(dc + 1) * 128, :],
                  acc[:, dc * 1024:(dc + 1) * 1024],
                  waits=[(s_V, s_V.n)], inc=(s_o, 16))
    b.emit("gpsimd", lambda e, v=s_o.n: e.wait_ge(sems["s_o"], v))

    with nc.Block() as block:
        @block.sync
        def _(e):
            for f in b.ops["sync"]:
                f(e)

        @block.gpsimd
        def _(e):
            for f in b.ops["gpsimd"]:
                f(e)

        @block.vector
        def _(e):
            for f in b.ops["vector"]:
                f(e)

        @block.scalar
        def _(e):
            for f in b.ops["scalar"]:
                f(e)

        @block.tensor
        def _(e):
            for f in b.ops["tensor"]:
                f(e)

    return nc



def make_in_maps(inputs):
    x = np.asarray(inputs["L0_attn_hidden_states"], dtype=np.float32)
    blob = build_blob(inputs)
    bias = np.asarray(inputs["L0_ffn_bias"], dtype=np.float32)
    gate = np.asarray(inputs["L0_attn_gate"], dtype=np.float32)
    in_maps = []
    for c in range(8):
        seq, half = c // 2, c % 2
        own = x[seq, half * 1024:(half + 1) * 1024, :]
        oth = x[seq, (1 - half) * 1024:(2 - half) * 1024, :]
        xTc = np.ascontiguousarray(np.concatenate([oth, own], axis=0).T)
        cpage = np.zeros((128, 32), dtype=np.float32)
        cpage[0:16, 0] = bias
        cpage[0, 2:18] = gate
        cpage[0, 20] = float(half == 1)
        in_maps.append({"xT": xTc, "blob": blob, "cpage": cpage})
    return in_maps


def kernel(**inputs):
    in_maps = make_in_maps(inputs)
    nc = build_nc()
    res = run_bass_kernel_spmd(nc, in_maps, core_ids=list(range(8)),
                               cleanup_stale_runs=True)
    out = np.empty((4, 2048, 2048), dtype=np.float32)
    for c in range(8):
        seq, half = c // 2, c % 2
        out[seq, half * 1024:(half + 1) * 1024, :] = res.results[c]["outT"].T
    return out


# version 08
# speedup vs baseline: 1.0251x; 1.0251x over previous
import numpy as np
import concourse.bass as bass
import concourse.mybir as mybir
from concourse.bass_utils import run_bass_kernel_spmd

F32 = mybir.dt.float32
ALU = mybir.AluOpType
AF = mybir.ActivationFunctionType
AX = mybir.AxisListType

D = 2048
DC = 16
H = 16
HD = 128
SCALE = HD ** -0.5
EPS_ATTN = 1e-6
EPS_LN = 1e-5
EPS_TOPK = 1e-9
IC_FF = 8
NBUF = 3

ENGS = ("sync", "gpsimd", "vector", "scalar", "tensor")
PHASES = ("init", "ln_oth", "state", "ln_own", "attn", "router", "experts")


BLOB_ITEMS = [
    ("ident", (128, 128)),
    ("onec", (128, 1)),
    ("onerow", (1, 128)),
    ("masku", (128, 128)),
    ("masku4", (128, 512)),
    ("sel4", (4, 512)),
    ("ones4", (4, 4)),
    ("mn2", (4, 2048)),
    ("wrT", (128, 256)),
    ("mnT", (128, 16, 4)),
    ("memT", (16, 128, 512)),
    ("wqT", (16, 128, 2048)),
    ("wkT", (16, 128, 2048)),
    ("wvT", (16, 128, 2048)),
    ("woT", (16, 128, 2048)),
    ("swg", (8, 128, 2048)),
    ("swu", (8, 128, 2048)),
    ("swd", (16, 128, 1024)),
    ("wg", (16, 8, 128, 2048)),
    ("wu", (16, 8, 128, 2048)),
    ("wd", (16, 16, 128, 1024)),
]


def blob_offsets():
    offs, o = {}, 0
    for name, shape in BLOB_ITEMS:
        n = 1
        for s in shape:
            n *= s
        offs[name] = (o, shape)
        o += n
    return offs, o


BLOB_OFFS, BLOB_N = blob_offsets()


def boff(name, *idx):
    off, shape = BLOB_OFFS[name]
    rem = 1
    for s in shape[len(idx):]:
        rem *= s
    flat = 0
    for k in range(len(idx)):
        flat = flat * shape[k] + idx[k]
    return off + flat * rem


def _strips(W):
    K, N = W.shape
    Kc, Nc = K // 128, N // 128
    return np.ascontiguousarray(
        W.reshape(Kc, 128, Nc, 128).transpose(2, 1, 0, 3).reshape(Nc, 128, Kc * 128))


def _strips_small(W):
    K, N = W.shape
    Kc = K // 128
    return np.ascontiguousarray(W.reshape(Kc, 128, N).transpose(1, 0, 2).reshape(128, Kc * N))


def build_blob(i):
    f = np.float32
    parts = {}
    parts["ident"] = np.eye(128, dtype=f)
    parts["onec"] = np.ones((128, 1), dtype=f)
    parts["onerow"] = np.ones((1, 128), dtype=f)
    parts["masku"] = np.triu(np.ones((128, 128), dtype=f))
    parts["masku4"] = np.tile(parts["masku"], (1, 4))
    s4 = np.zeros((4, 512), dtype=f)
    for m in range(4):
        s4[m, m * 128:(m + 1) * 128] = 1.0
    parts["sel4"] = s4
    parts["ones4"] = np.ones((4, 4), dtype=f)
    mn = np.asarray(i["L0_attn_memory_norms"], dtype=f)
    parts["mn2"] = np.ascontiguousarray(mn.reshape(4, 2048))
    parts["wrT"] = _strips_small(np.ascontiguousarray(np.asarray(i["L0_ffn_Wr"], f).T))
    parts["mnT"] = np.ascontiguousarray(mn.transpose(2, 1, 0))
    mem = np.asarray(i["L0_attn_memories"], dtype=f)
    parts["memT"] = np.ascontiguousarray(mem.transpose(1, 2, 0, 3).reshape(16, 128, 512))
    for nm, key in (("wqT", "L0_attn_w_q"), ("wkT", "L0_attn_w_k"),
                    ("wvT", "L0_attn_w_v"), ("woT", "L0_attn_w_o")):
        parts[nm] = _strips(np.ascontiguousarray(np.asarray(i[key], f).T))
    parts["swg"] = _strips(np.asarray(i["L0_ffn_sWg"], f))
    parts["swu"] = _strips(np.asarray(i["L0_ffn_sWu"], f))
    parts["swd"] = _strips(np.asarray(i["L0_ffn_sWd"], f))
    parts["wg"] = np.stack([_strips(np.asarray(i["L0_ffn_Wg"], f)[e]) for e in range(16)])
    parts["wu"] = np.stack([_strips(np.asarray(i["L0_ffn_Wu"], f)[e]) for e in range(16)])
    parts["wd"] = np.stack([_strips(np.asarray(i["L0_ffn_Wd"], f)[e]) for e in range(16)])
    blob = np.empty(BLOB_N, dtype=f)
    for name, (off, shape) in BLOB_OFFS.items():
        a = parts[name]
        assert tuple(a.shape) == tuple(shape), (name, a.shape, shape)
        blob[off:off + a.size] = a.ravel()
    return blob



class Ch:
    __slots__ = ("sem", "n", "step")

    def __init__(self, sem, step=1):
        self.sem = sem
        self.n = 0
        self.step = step


class Builder:
    def __init__(self, nc):
        self.nc = nc
        self.spacer = None
        self.ops = {e: [] for e in ENGS}

    @staticmethod
    def _tiny(ap):
        try:
            pat = ap.ap
            parts = pat[0][1]
            free = 1
            for sc in pat[1:]:
                free *= sc[1]
            return parts < 128 or free < 128
        except Exception:
            return True

    def _vspace(self, inc):
        sp = self.spacer
        self.emit("vector", lambda e, o=sp[0], i=sp[1]: e.tensor_copy(o, i))
        self.emit("vector", lambda e, o=sp[0], i=sp[1]: e.tensor_copy(o, i),
                  (), inc)

    def _aspace(self, inc):
        sp = self.aspacer
        self.emit("scalar", lambda e, o=sp[0], i=sp[1]: e.copy(o, i))
        self.emit("scalar", lambda e, o=sp[0], i=sp[1]: e.copy(o, i),
                  (), inc)

    def emit(self, eng, fn, waits=(), inc=None):
        seen = {}
        for ch, v in waits:
            if v is None or v <= 0:
                continue
            assert v <= ch.n, f"wait target {v} > emitted {ch.n} ({eng})"
            if id(ch) in seen and seen[id(ch)][1] >= v:
                continue
            seen[id(ch)] = (ch, v)
        wl = list(seen.values())
        il = None
        if inc is not None:
            ch, k = inc
            ch.n += k
            il = (ch.sem, k)

        def g(E, wl=wl, il=il, fn=fn):
            for ch, v in wl:
                E.wait_ge(ch.sem, v)
            ins = fn(E)
            if il is not None:
                ins.then_inc(il[0], il[1])
        self.ops[eng].append(g)

    def mm(self, out, lhsT, rhs, start, stop, waits=(), inc=None):
        self.emit("tensor",
                  lambda t, o=out, l=lhsT, r=rhs, a=start, bb=stop:
                  t.matmul(o, l, r, start=a, stop=bb), waits, inc)

    def tr(self, out, in_, ident, waits=(), inc=None):
        self.emit("tensor",
                  lambda t, o=out, i=in_, d=ident: t.transpose(o, i, d), waits, inc)

    def tt(self, out, a, b_, op, waits=(), inc=None):
        t = self._tiny(out)
        self.emit("vector",
                  lambda e, o=out, x=a, y=b_, p=op: e.tensor_tensor(o, x, y, p),
                  waits, None if t else inc)
        if t:
            self._vspace(inc)

    def ts(self, out, a, s1, op0, s2=None, op1=None, waits=(), inc=None):
        t = self._tiny(out)
        if op1 is None:
            self.emit("vector", lambda e, o=out, x=a, s=s1, p=op0:
                      e.tensor_scalar(o, x, s, None, p), waits, None if t else inc)
        else:
            self.emit("vector", lambda e, o=out, x=a, sa=s1, sb=s2, p0=op0, p1=op1:
                      e.tensor_scalar(o, x, sa, sb, p0, p1), waits, None if t else inc)
        if t:
            self._vspace(inc)

    def stt(self, out, a, s, b_, op0, op1, waits=(), inc=None):
        t = self._tiny(out)
        self.emit("vector", lambda e, o=out, x=a, sc=s, y=b_, p0=op0, p1=op1:
                  e.scalar_tensor_tensor(o, x, sc, y, p0, p1), waits,
                  None if t else inc)
        if t:
            self._vspace(inc)

    def act(self, out, in_, func, bias=None, scale=None, waits=(), inc=None):
        t = self._tiny(out)

        def f(e, o=out, i=in_, fu=func, bi=bias, s=scale):
            kw = {}
            if bi is not None:
                kw["bias"] = bi
            if s is not None:
                kw["scale"] = s
            return e.activation(o, i, fu, **kw)
        self.emit("scalar", f, waits, None if t else inc)
        if t:
            self._aspace(inc)

    def red(self, out, in_, op, waits=(), inc=None):
        self.emit("vector", lambda e, o=out, i=in_, p=op:
                  e.tensor_reduce(o, i, AX.X, p), waits, None)
        sp = self.spacer
        self.emit("vector", lambda e, o=sp[0], i=sp[1]: e.tensor_copy(o, i))
        self.emit("vector", lambda e, o=sp[0], i=sp[1]: e.tensor_copy(o, i),
                  (), inc)

    def recip(self, out, in_, waits=(), inc=None):
        t = self._tiny(out)
        self.emit("vector", lambda e, o=out, i=in_: e.reciprocal(o, i), waits,
                  None if t else inc)
        if t:
            self._vspace(inc)

    def cp(self, out, in_, waits=(), inc=None):
        t = self._tiny(out)
        self.emit("vector", lambda e, o=out, i=in_: e.tensor_copy(o, i), waits,
                  None if t else inc)
        if t:
            self._vspace(inc)

    def memset(self, ap, val, waits=(), inc=None):
        self.emit("vector", lambda e, a=ap, v=val: e.memset(a, v), waits, inc)

    def dma(self, eng, out, in_, waits=(), inc=None):
        self.emit(eng, lambda e, o=out, i=in_: e.dma_start(out=o, in_=i), waits, inc)




def build_nc(dump=None, upto=None):
    if upto == "attn1":
        run = PHASES[:PHASES.index("ln_own") + 1]
    else:
        run = PHASES if upto is None else PHASES[:PHASES.index(upto) + 1]
    nc = bass.Bass("TRN2", target_bir_lowering=False)
    xT = nc.dram_tensor("xT", [D, 2048], F32, kind="ExternalInput").ap()
    blob = nc.dram_tensor("blob", [BLOB_N], F32, kind="ExternalInput").ap()
    cpage_h = nc.dram_tensor("cpage", [128, 32], F32, kind="ExternalInput").ap()
    outT = nc.dram_tensor("outT", [D, 1024], F32, kind="ExternalOutput").ap()
    dbg = None
    if dump is not None:
        dbg = nc.dram_tensor("dbg", [128, 16384], F32, kind="ExternalOutput").ap()

    blob_t = blob.tensor

    AR_N = 8192
    xn = nc.alloc_sbuf_tensor("xn", [128, DC * 1024], F32).ap()
    acc = nc.alloc_sbuf_tensor("acc", [128, DC * 1024], F32).ap()
    ar = nc.alloc_sbuf_tensor("ar", [128, AR_N], F32).ap()
    ring = nc.alloc_sbuf_tensor("ring", [128, NBUF * 2048], F32).ap()
    S_sb = nc.alloc_sbuf_tensor("S_sb", [128, H * 128], F32).ap()
    Z_sb = nc.alloc_sbuf_tensor("Z_sb", [128, 16], F32).ap()
    mxn = nc.alloc_sbuf_tensor("mxn", [128, 32], F32).ap()
    mnT_sb = nc.alloc_sbuf_tensor("mnT_sb", [128, 64], F32).ap()
    r0 = nc.alloc_sbuf_tensor("r0", [1, 1536], F32).ap()
    sm = nc.alloc_sbuf_tensor("sm", [128, 32], F32).ap()
    wgtg = nc.alloc_sbuf_tensor("wgtg", [4, 16], F32).ap()
    idn = nc.alloc_sbuf_tensor("idn", [128, 128], F32).ap()
    msku4 = nc.alloc_sbuf_tensor("msku4", [128, 512], F32).ap()
    sel4 = nc.alloc_sbuf_tensor("sel4", [4, 512], F32).ap()
    cpg = nc.alloc_sbuf_tensor("cpg", [128, 32], F32).ap()
    sgr = nc.alloc_sbuf_tensor("sgr", [1, 48], F32).ap()
    sgb = nc.alloc_sbuf_tensor("sgb", [128, 16], F32).ap()
    on4 = nc.alloc_sbuf_tensor("on4", [4, 4], F32).ap()
    onr = nc.alloc_sbuf_tensor("onr", [1, 128], F32).ap()

    pb = [nc.alloc_psum_tensor(f"pb{i}", [128, 512], F32).ap() for i in range(8)]

    sems = {n: nc.alloc_semaphore(n) for n in
            ("s_w", "s_x", "s_o", "s_T", "s_V", "s_A")}
    s_w = Ch(sems["s_w"], 16)
    s_x = Ch(sems["s_x"], 16)
    s_o = Ch(sems["s_o"], 16)
    s_T = Ch(sems["s_T"])
    s_V = Ch(sems["s_V"])
    s_A = Ch(sems["s_A"])

    cwT = S_sb[0:16, 1024:2048]
    b = Builder(nc)
    b.spacer = (sm[:, 24:25], sm[:, 31:32])
    b.aspacer = (sm[:, 23:24], sm[:, 31:32])

    def bap(off, P, F):
        return bass.AP(blob_t, off, [[F, P], [1, F]])

    class WS:
        def __init__(self):
            self.cnt = 0
            self.freed = []

        def load(self, off, cols, rows=128):
            i = self.cnt
            slot = i % NBUF
            waits = []
            if i >= NBUF:
                waits.append((s_T, self.freed[i - NBUF]))
            b.dma("sync", ring[0:rows, slot * 2048: slot * 2048 + cols],
                  bap(off, rows, cols), waits=waits, inc=(s_w, 16))
            self.cnt += 1
            if i < NBUF:
                v = 16 * self.cnt
                b.emit("sync", lambda e, vv=v: e.wait_ge(s_w.sem, vv))
            return slot, 16 * self.cnt

        def done(self):
            self.freed.append(s_T.n)

    ws = WS()

    class XS:
        def __init__(self):
            self.cnt = 0
            self.freed = []

        def load(self, dst, src, depth=2, waits=()):
            i = self.cnt
            wl = list(waits)
            if i >= depth:
                wl += self.freed[i - depth]
            b.dma("gpsimd", dst, src, waits=wl, inc=(s_x, 16))
            self.cnt += 1
            if i < depth:
                v = s_x.n
                b.emit("gpsimd", lambda e, vv=v: e.wait_ge(s_x.sem, vv))

        def done(self, *chs):
            self.freed.append([(c, c.n) for c in chs])

    LN = dict(XS0=0, XS1=1024, SQA=2048, AB=3072, BB=4096,
              R1=5120, R2=6144, R3=7168)
    P2 = dict(KPT=0, KTK=1024, VT=2048, VTK=3072, S1=4096, S2=4608)
    AT = dict(ATTNB=0, SQ=4096, KPT=4608, KTK=5120, VTK=5632,
              S1=6144, S2=6656, SMT=7168, RBM=7296)
    RT = dict(LGT=0, PRB=1024, T0=1040, T1=1056, SCR=1072)
    FF = dict(SU=0)
    R0 = dict(CROW=0, DR=1024)

    one_col = sm[:, 31:32]
    mcol = sm[:, 30:31]
    em4 = sm[:, 28:29]
    tmpc = sm[:, 27:28]
    mq_sb = sm[:, 26:27]
    c1s = sm[:, 25:26]
    m1s = sm[:, 21:22]
    m2s = sm[:, 22:23]
    naec = sm[0:4, 15:16]
    brwc = sm[0:4, 16:17]
    rlc = sm[0:4, 11:12]
    rlc2 = sm[0:4, 12:13]
    scol = sm[0:4, 13:14]
    sc2 = sm[0:4, 14:15]

    def init_phase():
        b.dma("gpsimd", sm[:, 31:32], bap(boff("onec"), 128, 1), inc=(s_x, 16))
        b.dma("gpsimd", onr[0:1, :], bap(boff("onerow"), 1, 128), inc=(s_x, 16))
        b.dma("gpsimd", cpg[:, :], cpage_h[:, :], inc=(s_x, 16))
        b.dma("gpsimd", idn[:, :], bap(boff("ident"), 128, 128), inc=(s_x, 16))
        b.dma("gpsimd", msku4[:, :], bap(boff("masku4"), 128, 512), inc=(s_x, 16))
        b.dma("gpsimd", sel4[:, :], bap(boff("sel4"), 4, 512), inc=(s_x, 16))
        b.dma("gpsimd", on4[:, :], bap(boff("ones4"), 4, 4), inc=(s_x, 16))
        b.dma("gpsimd", mnT_sb[:, :], bap(boff("mnT"), 128, 64), inc=(s_x, 16))
        b.dma("gpsimd", ar[0:4, 0:2048], bap(boff("mn2"), 4, 2048), inc=(s_x, 16))
        xv0 = s_x.n
        b.emit("gpsimd", lambda e, vv=xv0: e.wait_ge(s_x.sem, vv))
        b.act(sgr[0:1, 0:16], cpg[0:1, 2:18], AF.Sigmoid,
              waits=[(s_x, xv0)], inc=(s_A, 1))
        b.act(sgr[0:1, 16:32], sgr[0:1, 0:16], AF.Copy, bias=1.0, scale=-1.0,
              waits=[(s_A, s_A.n)], inc=(s_A, 1))
        b.mm(pb[4][:, 0:2], onr[0:1, :], cpg[0:1, 20:22], True, True,
             waits=[(s_x, xv0), (s_V, s_V.n)], inc=(s_T, 1))
        b.cp(mcol, pb[4][:, 0:1], waits=[(s_T, s_T.n)], inc=(s_V, 1))
        b.mm(pb[5][:, 0:16], onr[0:1, :], sgr[0:1, 0:16], True, True,
             waits=[(s_A, s_A.n), (s_V, s_V.n)], inc=(s_T, 1))
        b.cp(sgb[:, :], pb[5][:, 0:16], waits=[(s_T, s_T.n)], inc=(s_V, 1))
        b.red(em4[0:4, 0:1], ar[0:4, 0:2048], ALU.add,
              waits=[(s_x, xv0)], inc=(s_V, 1))
        b.ts(em4[0:4, 0:1], em4[0:4, 0:1], 1e-6, ALU.is_lt,
             waits=[(s_V, s_V.n)], inc=(s_V, 1))
        b.mm(pb[3][0:4, 0:2], on4[0:4, 0:4], sm[0:4, 28:30], True, True,
             waits=[(s_V, s_V.n), (s_x, xv0)], inc=(s_T, 1))
        b.ts(naec, pb[3][0:4, 0:1], 3.5, ALU.is_gt,
             waits=[(s_T, s_T.n)], inc=(s_V, 1))
        b.ts(naec, naec, -1.0, ALU.mult, inc=(s_V, 1))
        b.ts(naec, naec, 1.0, ALU.add, inc=(s_V, 1))
        b.ts(brwc, em4[0:4, 0:1], -1e30, ALU.mult, inc=(s_V, 1))

    def ln_phase(col0, own):
        xs = XS()
        for dc in range(DC):
            slot = LN["XS0"] if dc % 2 == 0 else LN["XS1"]
            xt = ar[:, slot:slot + 1024]
            wl = [(s_V, s_V.n), (s_T, s_T.n)] if dc < 2 else ()
            xs.load(xt, bass.AP(xT.tensor, dc * 128 * 2048 + col0,
                                [[2048, 128], [1, 1024]]), waits=wl)
            sq = ar[:, LN["SQA"]:LN["SQA"] + 1024]
            b.act(sq, xt, AF.Square, waits=[(s_x, s_x.n), (s_T, s_T.n)],
                  inc=(s_A, 1))
            for g in range(2):
                b.mm(pb[g][0:1, :], one_col, xt[:, g * 512:(g + 1) * 512],
                     dc == 0, dc == DC - 1,
                     waits=[(s_x, s_x.n)] if g == 0 else (),
                     inc=(s_T, 1) if g == 1 else None)
            for g in range(2):
                b.mm(pb[2 + g][0:1, :], one_col, sq[:, g * 512:(g + 1) * 512],
                     dc == 0, dc == DC - 1,
                     waits=[(s_A, s_A.n)] if g == 0 else (),
                     inc=(s_T, 1) if g == 1 else None)
            xs.done(s_T)
        R1 = ar[0:1, LN["R1"]:LN["R1"] + 1024]
        R2 = ar[0:1, LN["R2"]:LN["R2"] + 1024]
        R3 = ar[0:1, LN["R3"]:LN["R3"] + 1024]
        for g in range(2):
            b.ts(R1[0:1, g * 512:(g + 1) * 512], pb[g][0:1, :], 1.0 / D,
                 ALU.mult, waits=[(s_T, s_T.n)], inc=(s_V, 1))
            b.ts(R2[0:1, g * 512:(g + 1) * 512], pb[2 + g][0:1, :], 1.0 / D,
                 ALU.mult, inc=(s_V, 1))
        b.tt(R3, R1, R1, ALU.mult, inc=(s_V, 1))
        b.tt(R2, R2, R3, ALU.subtract, inc=(s_V, 1))
        b.ts(R2, R2, EPS_LN, ALU.add, inc=(s_V, 1))
        b.recip(R3, R2, inc=(s_V, 1))
        b.act(R2, R3, AF.Sqrt, waits=[(s_V, s_V.n)], inc=(s_A, 1))
        b.stt(R3, R1, -1.0, R2, ALU.mult, ALU.mult,
              waits=[(s_A, s_A.n)], inc=(s_V, 1))
        for half in range(2):
            b.mm(pb[4 + half][:, :], onr[0:1, :],
                 R2[0:1, half * 512:(half + 1) * 512], True, True,
                 waits=[(s_A, s_A.n), (s_V, s_V.n)], inc=(s_T, 1))
            b.cp(ar[:, LN["AB"] + half * 512:LN["AB"] + (half + 1) * 512],
                 pb[4 + half][:, :], waits=[(s_T, s_T.n)], inc=(s_V, 1))
            b.mm(pb[6 + half][:, :], onr[0:1, :],
                 R3[0:1, half * 512:(half + 1) * 512], True, True,
                 waits=[(s_V, s_V.n)], inc=(s_T, 1))
            b.cp(ar[:, LN["BB"] + half * 512:LN["BB"] + (half + 1) * 512],
                 pb[6 + half][:, :], waits=[(s_T, s_T.n)], inc=(s_V, 1))
        a_b = ar[:, LN["AB"]:LN["AB"] + 1024]
        b_b = ar[:, LN["BB"]:LN["BB"] + 1024]
        xs2 = XS()
        for dc in range(DC):
            if own:
                dst = acc[:, dc * 1024:(dc + 1) * 1024]
            else:
                base = LN["XS0"] if dc % 2 == 0 else LN["XS1"]
                dst = ar[:, base:base + 1024]
            wl2 = [(s_T, s_T.n), (s_A, s_A.n)] if (dc < 2 and not own) else ()
            xs2.load(dst, bass.AP(xT.tensor, dc * 128 * 2048 + col0,
                                  [[2048, 128], [1, 1024]]), waits=wl2)
            xnt = xn[:, dc * 1024:(dc + 1) * 1024]
            b.tt(xnt, dst, a_b, ALU.mult, waits=[(s_x, s_x.n)], inc=(s_V, 1))
            b.tt(xnt, xnt, b_b, ALU.add, inc=(s_V, 1))
            b.red(tmpc, xnt, ALU.add, inc=(s_V, 1))
            if own:
                b.tt(mxn[:, dc:dc + 1], mxn[:, dc:dc + 1], tmpc, ALU.add,
                     inc=(s_V, 1))
            else:
                b.cp(mxn[:, dc:dc + 1], tmpc, inc=(s_V, 1))
            xs2.done(s_V)

    def sigma(dst, src_psum, s1c, s2c, n):
        t1 = ar[:, s1c:s1c + n]
        t2 = ar[:, s2c:s2c + n]
        b.ts(t1, src_psum, 0.0, ALU.min, waits=[(s_T, s_T.n)], inc=(s_V, 1))
        b.act(t2, t1, AF.Exp, waits=[(s_V, s_V.n)], inc=(s_A, 1))
        b.stt(dst, src_psum, 0.0, t2, ALU.max, ALU.add,
              waits=[(s_A, s_A.n)], inc=(s_V, 1))

    def state_build():
        for h in range(H):
            slot, wv = ws.load(boff("wkT", h), 2048)
            base = slot * 2048
            for tg in range(2):
                for dc in range(DC):
                    b.mm(pb[4][:, :], ring[:, base + dc * 128:base + (dc + 1) * 128],
                         xn[:, dc * 1024 + tg * 512: dc * 1024 + (tg + 1) * 512],
                         dc == 0, dc == DC - 1,
                         waits=[(s_w, wv), (s_V, s_V.n)] if dc == 0 else (),
                         inc=(s_T, 1) if dc == DC - 1 else None)
                sigma(ar[:, P2["KPT"] + tg * 512: P2["KPT"] + (tg + 1) * 512],
                      pb[4][:, :], P2["S1"], P2["S2"], 512)
            ws.done()
            kpT = ar[:, P2["KPT"]:P2["KPT"] + 1024]
            b.red(tmpc, kpT, ALU.add, inc=(s_V, 1))
            b.ts(Z_sb[:, h:h + 1], tmpc, mcol, ALU.mult, inc=(s_V, 1))
            for q in range(2):
                for c in range(4):
                    cc = q * 4 + c
                    b.tr(pb[6][:, c * 128:(c + 1) * 128],
                         kpT[:, cc * 128:(cc + 1) * 128], idn[:, :],
                         waits=[(s_V, s_V.n)] if c == 0 else (),
                         inc=(s_T, 1) if c == 3 else None)
                for c in range(4):
                    cc = q * 4 + c
                    b.cp(ar[:, P2["KTK"] + cc * 128:P2["KTK"] + (cc + 1) * 128],
                         pb[6][:, c * 128:(c + 1) * 128],
                         waits=[(s_T, s_T.n)] if c == 0 else (), inc=(s_V, 1))
            slot, wv2 = ws.load(boff("wvT", h), 2048)
            base = slot * 2048
            for tg in range(2):
                for dc in range(DC):
                    b.mm(pb[5][:, :], ring[:, base + dc * 128:base + (dc + 1) * 128],
                         xn[:, dc * 1024 + tg * 512: dc * 1024 + (tg + 1) * 512],
                         dc == 0, dc == DC - 1,
                         waits=[(s_w, wv2), (s_V, s_V.n)] if dc == 0 else (),
                         inc=(s_T, 1) if dc == DC - 1 else None)
                b.cp(ar[:, P2["VT"] + tg * 512:P2["VT"] + (tg + 1) * 512],
                     pb[5][:, :], waits=[(s_T, s_T.n)], inc=(s_V, 1))
            ws.done()
            vT = ar[:, P2["VT"]:P2["VT"] + 1024]
            for q in range(2):
                for c in range(4):
                    cc = q * 4 + c
                    b.tr(pb[6][:, c * 128:(c + 1) * 128],
                         vT[:, cc * 128:(cc + 1) * 128], idn[:, :],
                         waits=[(s_V, s_V.n)] if c == 0 else (),
                         inc=(s_T, 1) if c == 3 else None)
                for c in range(4):
                    cc = q * 4 + c
                    b.cp(ar[:, P2["VTK"] + cc * 128:P2["VTK"] + (cc + 1) * 128],
                         pb[6][:, c * 128:(c + 1) * 128],
                         waits=[(s_T, s_T.n)] if c == 0 else (), inc=(s_V, 1))
            for c in range(8):
                b.mm(pb[7][:, 0:128],
                     ar[:, P2["KTK"] + c * 128:P2["KTK"] + (c + 1) * 128],
                     ar[:, P2["VTK"] + c * 128:P2["VTK"] + (c + 1) * 128],
                     c == 0, c == 7,
                     waits=[(s_V, s_V.n)] if c == 0 else (),
                     inc=(s_T, 1) if c == 7 else None)
            b.ts(S_sb[:, h * 128:(h + 1) * 128], pb[7][:, 0:128], mcol,
                 ALU.mult, waits=[(s_T, s_T.n)], inc=(s_V, 1))

    def attention(nth=2, nhp=2):
        for th in range(nth):
            tb = 512 * th
            for hp in range(nhp):
                for hh in range(8):
                    h = hp * 8 + hh
                    sq_ap = ar[:, AT["SQ"]:AT["SQ"] + 512]
                    kpT = ar[:, AT["KPT"]:AT["KPT"] + 512]
                    ktk = ar[:, AT["KTK"]:AT["KTK"] + 512]
                    vtk = ar[:, AT["VTK"]:AT["VTK"] + 512]
                    slot, wv = ws.load(boff("wqT", h), 2048)
                    base = slot * 2048
                    for dc in range(DC):
                        b.mm(pb[0][:, :], ring[:, base + dc * 128:base + (dc + 1) * 128],
                             xn[:, dc * 1024 + tb: dc * 1024 + tb + 512],
                             dc == 0, dc == DC - 1,
                             waits=[(s_w, wv), (s_V, s_V.n)] if dc == 0 else (),
                             inc=(s_T, 1) if dc == DC - 1 else None)
                    if th == 0:
                        for dc in range(DC):
                            b.mm(pb[3][:, 0:2],
                                 ring[:, base + dc * 128:base + (dc + 1) * 128],
                                 mxn[:, dc:dc + 2], dc == 0, dc == DC - 1,
                                 inc=(s_T, 1) if dc == DC - 1 else None)
                    sigma(sq_ap, pb[0][:, :], AT["S1"], AT["S2"], 512)
                    ws.done()
                    if th == 0:
                        b.cp(mq_sb, pb[3][:, 0:1], waits=[(s_T, s_T.n)],
                             inc=(s_V, 1))
                        b.mm(pb[3][0:4, 4:6], mnT_sb[:, h * 4:(h + 1) * 4],
                             sm[:, 26:28], True, True, waits=[(s_V, s_V.n)],
                             inc=(s_T, 1))
                        b.ts(rlc, pb[3][0:4, 4:5], SCALE / 2048.0, ALU.mult,
                             waits=[(s_T, s_T.n)], inc=(s_V, 1))
                        b.tt(rlc, rlc, brwc, ALU.add, inc=(s_V, 1))
                        b.act(rlc2, rlc, AF.Exp, waits=[(s_V, s_V.n)],
                              inc=(s_A, 1))
                        b.mm(pb[3][0:4, 6:8], on4[0:4, 0:4], sm[0:4, 12:14],
                             True, True, waits=[(s_A, s_A.n)], inc=(s_T, 1))
                        b.ts(scol, pb[3][0:4, 6:7], 1e-30, ALU.add,
                             waits=[(s_T, s_T.n)], inc=(s_V, 1))
                        b.recip(sc2, scol, inc=(s_V, 1))
                        b.tt(wgtg[0:4, h:h + 1], rlc2, sc2, ALU.mult,
                             inc=(s_V, 1))
                        b.tt(wgtg[0:4, h:h + 1], wgtg[0:4, h:h + 1],
                             sgb[0:4, h:h + 1], ALU.mult, inc=(s_V, 1))
                        b.tt(wgtg[0:4, h:h + 1], wgtg[0:4, h:h + 1],
                             naec, ALU.mult, inc=(s_V, 1))
                    slot, wv = ws.load(boff("wkT", h), 2048)
                    base = slot * 2048
                    for dc in range(DC):
                        b.mm(pb[0][:, :], ring[:, base + dc * 128:base + (dc + 1) * 128],
                             xn[:, dc * 1024 + tb: dc * 1024 + tb + 512],
                             dc == 0, dc == DC - 1,
                             waits=[(s_w, wv), (s_V, s_V.n)] if dc == 0 else (),
                             inc=(s_T, 1) if dc == DC - 1 else None)
                    sigma(kpT, pb[0][:, :], AT["S1"], AT["S2"], 512)
                    ws.done()
                    for c in range(4):
                        b.tr(pb[4][:, c * 128:(c + 1) * 128],
                             kpT[:, c * 128:(c + 1) * 128], idn[:, :],
                             waits=[(s_V, s_V.n)] if c == 0 else (),
                             inc=(s_T, 1) if c == 3 else None)
                    for c in range(4):
                        b.cp(ktk[:, c * 128:(c + 1) * 128],
                             pb[4][:, c * 128:(c + 1) * 128],
                             waits=[(s_T, s_T.n)] if c == 0 else (), inc=(s_V, 1))
                    slot, wv = ws.load(boff("wvT", h), 2048)
                    base = slot * 2048
                    for c in range(4):
                        bank = pb[1] if c % 2 == 0 else pb[2]
                        for dc in range(DC):
                            b.mm(bank[:, 0:128],
                                 xn[:, dc * 1024 + tb + c * 128:
                                    dc * 1024 + tb + (c + 1) * 128],
                                 ring[:, base + dc * 128:base + (dc + 1) * 128],
                                 dc == 0, dc == DC - 1,
                                 waits=[(s_w, wv), (s_V, s_V.n)] if dc == 0 else (),
                                 inc=(s_T, 1) if dc == DC - 1 else None)
                        b.cp(vtk[:, c * 128:(c + 1) * 128], bank[:, 0:128],
                             waits=[(s_T, s_T.n)], inc=(s_V, 1))
                    ws.done()
                    SMT5 = AT["SMT"]
                    SV = AT["SMT"] + 512
                    zb = sm[:, 4:8]
                    zc = sm[:, 17:21]
                    for c in range(4):
                        b.mm(pb[6][:, c * 128:(c + 1) * 128],
                             ktk[:, c * 128:(c + 1) * 128],
                             vtk[:, c * 128:(c + 1) * 128], True, True,
                             waits=[(s_V, s_V.n)] if c == 0 else (),
                             inc=(s_T, 1) if c == 3 else None)
                    for c in range(4):
                        b.mm(pb[4][:, c * 128:(c + 1) * 128],
                             kpT[:, c * 128:(c + 1) * 128],
                             sq_ap[:, c * 128:(c + 1) * 128], True, True,
                             inc=(s_T, 1) if c == 3 else None)
                    smt5 = ar[:, SMT5:SMT5 + 512]
                    b.tt(smt5, pb[4][:, 0:512], msku4[:, :], ALU.mult,
                         waits=[(s_T, s_T.n)], inc=(s_V, 1))
                    kp3d = bass.AP(ar.tensor, AT["KPT"],
                                   [[AR_N, 128], [128, 4], [1, 128]])
                    b.red(zb, kp3d, ALU.add, inc=(s_V, 1))
                    b.cp(zc[:, 0:1], Z_sb[:, h:h + 1], inc=(s_V, 1))
                    for c in range(3):
                        b.tt(zc[:, c + 1:c + 2], zc[:, c:c + 1], zb[:, c:c + 1],
                             ALU.add, inc=(s_V, 1))
                    b.tt(Z_sb[:, h:h + 1], zc[:, 3:4], zb[:, 3:4], ALU.add,
                         inc=(s_V, 1))
                    sv = [S_sb[:, h * 128:(h + 1) * 128]]
                    for c in range(3):
                        dst = ar[:, SV + c * 128:SV + (c + 1) * 128]
                        b.tt(dst, sv[-1], pb[6][:, c * 128:(c + 1) * 128],
                             ALU.add, inc=(s_V, 1))
                        sv.append(dst)
                    sv4 = ar[:, SV + 384:SV + 512]
                    b.tt(sv4, sv[3], pb[6][:, 384:512], ALU.add, inc=(s_V, 1))
                    slot, wv = ws.load(boff("memT", h), 512)
                    base = slot * 2048
                    b.mm(pb[5][0:4, :], mnT_sb[:, h * 4:(h + 1) * 4], sq_ap,
                         True, True, waits=[(s_V, s_V.n)], inc=(s_T, 1))
                    coef = ar[0:4, AT["S1"]:AT["S1"] + 512]
                    b.ts(coef, pb[5][0:4, :], EPS_ATTN, ALU.max,
                         waits=[(s_T, s_T.n)], inc=(s_V, 1))
                    b.recip(coef, coef, inc=(s_V, 1))
                    b.ts(coef, coef, wgtg[0:4, h:h + 1], ALU.mult, inc=(s_V, 1))
                    for m in range(4):
                        b.mm(pb[6][:, :], sel4[0:4, m * 128:(m + 1) * 128],
                             coef, True, True,
                             waits=[(s_V, s_V.n)], inc=(s_T, 1))
                        sqs = ar[:, AT["S2"]:AT["S2"] + 512]
                        b.tt(sqs, sq_ap, pb[6][:, :], ALU.mult,
                             waits=[(s_T, s_T.n)], inc=(s_V, 1))
                        b.mm(pb[5][:, :], ring[:, base + m * 128:base + (m + 1) * 128],
                             sqs, m == 0, m == 3,
                             waits=[(s_w, wv), (s_V, s_V.n)], inc=(s_T, 1))
                    ws.done()
                    vready = s_V.n
                    b.mm(pb[7][0:1, 0:512], one_col, smt5, True, False,
                         waits=[(s_V, vready)])
                    for c in range(4):
                        b.mm(pb[7][0:1, c * 128:(c + 1) * 128], zc[:, c:c + 1],
                             sq_ap[:, c * 128:(c + 1) * 128], False, c == 3,
                             inc=(s_T, 1) if c == 3 else None)
                    nbk = [pb[0], pb[1], pb[2], pb[3]]
                    for c in range(4):
                        b.mm(nbk[c][:, 0:128], vtk[:, c * 128:(c + 1) * 128],
                             smt5[:, c * 128:(c + 1) * 128], True, False)
                        b.mm(nbk[c][:, 0:128], sv[c],
                             sq_ap[:, c * 128:(c + 1) * 128], False, True,
                             inc=(s_T, 1))
                    b.cp(S_sb[:, h * 128:(h + 1) * 128], sv4,
                         waits=[(s_T, s_T.n)], inc=(s_V, 1))
                    dr5 = r0[0:1, R0["DR"]:R0["DR"] + 512]
                    b.ts(dr5, pb[7][0:1, 0:512], EPS_ATTN, ALU.max,
                         waits=[(s_T, s_T.n)], inc=(s_V, 1))
                    b.recip(dr5, dr5, inc=(s_V, 1))
                    b.ts(dr5, dr5, sgr[0:1, 16 + h:17 + h], ALU.mult,
                         inc=(s_V, 1))
                    b.mm(pb[7][:, 0:512], onr[0:1, :], dr5, True, True,
                         waits=[(s_V, s_V.n)], inc=(s_T, 1))
                    rbm5 = ar[:, SMT5:SMT5 + 512]
                    b.cp(rbm5, pb[7][:, 0:512], waits=[(s_T, s_T.n)],
                         inc=(s_V, 1))
                    for c in range(4):
                        ab = ar[:, AT["ATTNB"] + hh * 512 + c * 128:
                                AT["ATTNB"] + hh * 512 + (c + 1) * 128]
                        b.tt(ab, nbk[c][:, 0:128],
                             rbm5[:, c * 128:(c + 1) * 128], ALU.mult,
                             inc=(s_V, 1))
                    abh = ar[:, AT["ATTNB"] + hh * 512:AT["ATTNB"] + (hh + 1) * 512]
                    b.tt(abh, abh, pb[5][:, :], ALU.add, waits=[(s_T, s_T.n)],
                         inc=(s_V, 1))
                for ec in range(DC):
                    slot, wv = ws.load(boff("woT", ec), 2048)
                    base = slot * 2048
                    bank = pb[0] if ec % 2 == 0 else pb[1]
                    for hh in range(8):
                        hc = hp * 8 + hh
                        b.mm(bank[:, :], ring[:, base + hc * 128:base + (hc + 1) * 128],
                             ar[:, AT["ATTNB"] + hh * 512:AT["ATTNB"] + (hh + 1) * 512],
                             hh == 0, hh == 7,
                             waits=[(s_w, wv), (s_V, s_V.n)] if hh == 0 else (),
                             inc=(s_T, 1) if hh == 7 else None)
                    ws.done()
                    at = acc[:, ec * 1024 + tb: ec * 1024 + tb + 512]
                    b.tt(at, at, bank[:, :], ALU.add, waits=[(s_T, s_T.n)],
                         inc=(s_V, 1))

    def router():
        slot, wv = ws.load(boff("wrT"), 256)
        base = slot * 2048
        for tg in range(2):
            for dc in range(DC):
                b.mm(pb[0][0:16, :], ring[:, base + dc * 16:base + (dc + 1) * 16],
                     xn[:, dc * 1024 + tg * 512: dc * 1024 + (tg + 1) * 512],
                     dc == 0, dc == DC - 1,
                     waits=[(s_w, wv), (s_V, s_V.n)] if dc == 0 else (),
                     inc=(s_T, 1) if dc == DC - 1 else None)
            b.ts(ar[0:16, RT["LGT"] + tg * 512:RT["LGT"] + (tg + 1) * 512],
                 pb[0][0:16, :], cpg[0:16, 0:1], ALU.add,
                 waits=[(s_T, s_T.n)], inc=(s_V, 1))
        ws.done()
        for i in range(8):
            lg = ar[0:16, RT["LGT"] + i * 128:RT["LGT"] + (i + 1) * 128]
            b.tr(pb[4][:, 0:16], lg, idn[0:16, 0:16], waits=[(s_V, s_V.n)],
                 inc=(s_T, 1))
            P = ar[:, RT["PRB"]:RT["PRB"] + 16]
            t0 = ar[:, RT["T0"]:RT["T0"] + 16]
            t1 = ar[:, RT["T1"]:RT["T1"] + 16]
            b.cp(P, pb[4][:, 0:16], waits=[(s_T, s_T.n)], inc=(s_V, 1))
            b.red(c1s, P, ALU.max, inc=(s_V, 1))
            b.ts(c1s, c1s, -1.0, ALU.mult, inc=(s_V, 1))
            b.act(P, P, AF.Exp, bias=c1s, waits=[(s_V, s_V.n)], inc=(s_A, 1))
            b.red(c1s, P, ALU.add, waits=[(s_A, s_A.n)], inc=(s_V, 1))
            b.recip(c1s, c1s, inc=(s_V, 1))
            b.ts(P, P, c1s, ALU.mult, inc=(s_V, 1))

            def first_mask(src, mcol_, out_mask):
                b.ts(t0, src, mcol_, ALU.is_equal, inc=(s_V, 1))
                cur, oth = t0, t1
                for k in (1, 2, 4, 8):
                    b.cp(oth[:, 0:k], cur[:, 0:k], inc=(s_V, 1))
                    b.tt(oth[:, k:16], cur[:, k:16], cur[:, 0:16 - k], ALU.add,
                         inc=(s_V, 1))
                    cur, oth = oth, cur
                b.ts(oth, src, mcol_, ALU.is_equal, inc=(s_V, 1))
                b.tt(cur, cur, oth, ALU.subtract, inc=(s_V, 1))
                b.ts(cur, cur, 0.0, ALU.is_equal, inc=(s_V, 1))
                b.tt(out_mask, cur, oth, ALU.mult, inc=(s_V, 1))

            fm1 = ar[:, RT["SCR"]:RT["SCR"] + 16]
            fm2 = ar[:, RT["SCR"] + 16:RT["SCR"] + 32]
            p2_ = ar[:, RT["SCR"] + 32:RT["SCR"] + 48]
            b.red(m1s, P, ALU.max, inc=(s_V, 1))
            first_mask(P, m1s, fm1)
            b.stt(p2_, fm1, -1e30, P, ALU.mult, ALU.add, inc=(s_V, 1))
            b.red(m2s, p2_, ALU.max, inc=(s_V, 1))
            first_mask(p2_, m2s, fm2)
            b.tt(fm1, fm1, fm2, ALU.add, inc=(s_V, 1))
            b.tt(m1s, m1s, m2s, ALU.add, inc=(s_V, 1))
            b.ts(m1s, m1s, EPS_TOPK, ALU.add, inc=(s_V, 1))
            b.recip(m1s, m1s, inc=(s_V, 1))
            b.tt(fm1, fm1, P, ALU.mult, inc=(s_V, 1))
            b.ts(fm1, fm1, m1s, ALU.mult, inc=(s_V, 1))
            b.tr(pb[4][0:16, 16:144], fm1, idn[:, :], waits=[(s_V, s_V.n)],
                 inc=(s_T, 1))
            b.cp(cwT[0:16, i * 128:(i + 1) * 128], pb[4][0:16, 16:144],
                 waits=[(s_T, s_T.n)], inc=(s_V, 1))

    def experts():
        bfree = {i: [(s_V, s_V.n), (s_A, s_A.n)] for i in range(8)}
        for e in range(-1, 16):
            routed = e >= 0
            if routed:
                crow = r0[0:1, R0["CROW"]:R0["CROW"] + 1024]
                b.dma("gpsimd", crow, cwT[e:e + 1, 0:1024],
                      waits=[(s_V, s_V.n)], inc=(s_x, 16))
                cxv = s_x.n
                for tg in range(2):
                    b.mm(pb[6 + tg][:, :], onr[0:1, :],
                         crow[0:1, tg * 512:(tg + 1) * 512], True, True,
                         waits=[(s_x, cxv)] + bfree[6 + tg], inc=(s_T, 1))
                    b.cp(S_sb[:, tg * 512:(tg + 1) * 512], pb[6 + tg][:, :],
                         waits=[(s_T, s_T.n)], inc=(s_V, 1))
                    bfree[6 + tg] = [(s_V, s_V.n)]
            for ic in range(IC_FF):
                if routed:
                    sg_, wvg = ws.load(boff("wg", e, ic), 2048)
                    su_, wvu = ws.load(boff("wu", e, ic), 2048)
                else:
                    sg_, wvg = ws.load(boff("swg", ic), 2048)
                    su_, wvu = ws.load(boff("swu", ic), 2048)
                bg, bu = sg_ * 2048, su_ * 2048
                gb = 0 if ic % 2 == 0 else 4
                ub = 2 if ic % 2 == 0 else 6
                for tg in range(2):
                    for dc in range(DC):
                        b.mm(pb[gb + tg][:, :], ring[:, bg + dc * 128:bg + (dc + 1) * 128],
                             xn[:, dc * 1024 + tg * 512: dc * 1024 + (tg + 1) * 512],
                             dc == 0, dc == DC - 1,
                             waits=([(s_w, wvg)] + bfree[gb + tg]) if dc == 0 else (),
                             inc=(s_T, 1) if dc == DC - 1 else None)
                gv = s_T.n
                for tg in range(2):
                    for dc in range(DC):
                        b.mm(pb[ub + tg][:, :], ring[:, bu + dc * 128:bu + (dc + 1) * 128],
                             xn[:, dc * 1024 + tg * 512: dc * 1024 + (tg + 1) * 512],
                             dc == 0, dc == DC - 1,
                             waits=([(s_w, wvu)] + bfree[ub + tg]) if dc == 0 else (),
                             inc=(s_T, 1) if dc == DC - 1 else None)
                uv = s_T.n
                for tg in range(2):
                    sut = ar[:, FF["SU"] + ic * 1024 + tg * 512:
                             FF["SU"] + ic * 1024 + (tg + 1) * 512]
                    b.act(sut, pb[gb + tg][:, :], AF.Silu, waits=[(s_T, gv)],
                          inc=(s_A, 1))
                    bfree[gb + tg] = [(s_A, s_A.n)]
                    b.tt(sut, sut, pb[ub + tg][:, :], ALU.mult,
                         waits=[(s_A, s_A.n), (s_T, uv)], inc=(s_V, 1))
                    bfree[ub + tg] = [(s_V, s_V.n)]
                    if routed:
                        b.tt(sut, sut, S_sb[:, tg * 512:(tg + 1) * 512],
                             ALU.mult, inc=(s_V, 1))
                ws.done()
                ws.done()
            for hc in range(DC):
                if routed:
                    sd, wvd = ws.load(boff("wd", e, hc), 1024)
                else:
                    sd, wvd = ws.load(boff("swd", hc), 1024)
                bd = sd * 2048
                for tg in range(2):
                    bi = (hc * 2 + tg) % 4
                    bank = pb[bi]
                    for ic in range(IC_FF):
                        b.mm(bank[:, :], ring[:, bd + ic * 128:bd + (ic + 1) * 128],
                             ar[:, FF["SU"] + ic * 1024 + tg * 512:
                                FF["SU"] + ic * 1024 + (tg + 1) * 512],
                             ic == 0, ic == IC_FF - 1,
                             waits=([(s_w, wvd)] + bfree[bi]) if ic == 0 else (),
                             inc=(s_T, 1) if ic == IC_FF - 1 else None)
                    at = acc[:, hc * 1024 + tg * 512: hc * 1024 + (tg + 1) * 512]
                    b.tt(at, at, bank[:, :], ALU.add, waits=[(s_T, s_T.n)],
                         inc=(s_V, 1))
                    bfree[bi] = [(s_V, s_V.n)]
                ws.done()

    init_phase()
    if "ln_oth" in run:
        ln_phase(0, own=False)
    if "state" in run:
        state_build()
    if "ln_own" in run:
        ln_phase(1024, own=True)
    if upto == "attn1":
        attention(nth=1, nhp=1)
    elif "attn" in run:
        attention()
    if "router" in run:
        router()
    if "experts" in run:
        experts()

    def dump_all(ap, cols):
        b.dma("gpsimd", dbg[:, 0:cols], ap,
              waits=[(s_V, s_V.n), (s_A, s_A.n), (s_T, s_T.n)], inc=(s_o, 16))

    if dump == "xn":
        dump_all(xn[:, :], 16384)
    elif dump == "acc":
        dump_all(acc[:, :], 16384)
    elif dump == "state":
        dump_all(S_sb[:, :], 2048)
        b.dma("gpsimd", dbg[:, 2048:2064], Z_sb[:, :], inc=(s_o, 16))
        b.dma("gpsimd", dbg[:, 2064:2080], mxn[:, :], inc=(s_o, 16))
        b.dma("gpsimd", dbg[:, 2080:2112], sm[:, :], inc=(s_o, 16))
    elif dump == "cw":
        b.dma("gpsimd", dbg[0:16, 0:1024], cwT[:, :],
              waits=[(s_V, s_V.n), (s_T, s_T.n)], inc=(s_o, 16))
        b.dma("gpsimd", dbg[0:4, 2048:2064], wgtg[:, :], inc=(s_o, 16))
        b.dma("gpsimd", dbg[0:1, 2064:2112], sgr[:, :], inc=(s_o, 16))
        b.dma("gpsimd", dbg[:, 2112:2144], sm[:, :], inc=(s_o, 16))
    elif dump == "arena":
        dump_all(ar[:, :], AR_N)
        b.dma("gpsimd", dbg[0:4, 8192:8208], wgtg[:, :], inc=(s_o, 16))
        b.dma("gpsimd", dbg[0:1, 8208:8256], sgr[:, :], inc=(s_o, 16))
        b.dma("gpsimd", dbg[:, 8256:8272], Z_sb[:, :], inc=(s_o, 16))
        b.dma("gpsimd", dbg[:, 8272:8304], sm[:, :], inc=(s_o, 16))
        b.dma("gpsimd", dbg[:, 8304:10352], S_sb[:, :], inc=(s_o, 16))
        b.dma("gpsimd", dbg[:, 10352:10384], mxn[:, :], inc=(s_o, 16))

    if upto is None:
        for dc in range(DC):
            b.dma("gpsimd", outT[dc * 128:(dc + 1) * 128, :],
                  acc[:, dc * 1024:(dc + 1) * 1024],
                  waits=[(s_V, s_V.n)], inc=(s_o, 16))
    b.emit("gpsimd", lambda e, v=s_o.n: e.wait_ge(sems["s_o"], v))

    with nc.Block() as block:
        @block.sync
        def _(e):
            for f in b.ops["sync"]:
                f(e)

        @block.gpsimd
        def _(e):
            for f in b.ops["gpsimd"]:
                f(e)

        @block.vector
        def _(e):
            for f in b.ops["vector"]:
                f(e)

        @block.scalar
        def _(e):
            for f in b.ops["scalar"]:
                f(e)

        @block.tensor
        def _(e):
            for f in b.ops["tensor"]:
                f(e)

    return nc



def make_in_maps(inputs):
    x = np.asarray(inputs["L0_attn_hidden_states"], dtype=np.float32)
    blob = build_blob(inputs)
    bias = np.asarray(inputs["L0_ffn_bias"], dtype=np.float32)
    gate = np.asarray(inputs["L0_attn_gate"], dtype=np.float32)
    in_maps = []
    for c in range(8):
        seq, half = c // 2, c % 2
        own = x[seq, half * 1024:(half + 1) * 1024, :]
        oth = x[seq, (1 - half) * 1024:(2 - half) * 1024, :]
        xTc = np.ascontiguousarray(np.concatenate([oth, own], axis=0).T)
        cpage = np.zeros((128, 32), dtype=np.float32)
        cpage[0:16, 0] = bias
        cpage[0, 2:18] = gate
        cpage[0, 20] = float(half == 1)
        in_maps.append({"xT": xTc, "blob": blob, "cpage": cpage})
    return in_maps


def kernel(**inputs):
    in_maps = make_in_maps(inputs)
    nc = build_nc()
    res = run_bass_kernel_spmd(nc, in_maps, core_ids=list(range(8)),
                               cleanup_stale_runs=True)
    out = np.empty((4, 2048, 2048), dtype=np.float32)
    for c in range(8):
        seq, half = c // 2, c % 2
        out[seq, half * 1024:(half + 1) * 1024, :] = res.results[c]["outT"].T
    return out


# version 09 (final)
# speedup vs baseline: 1.1136x; 1.0863x over previous
import numpy as np
import concourse.bass as bass
import concourse.mybir as mybir
from concourse.bass_utils import run_bass_kernel_spmd

F32 = mybir.dt.float32
ALU = mybir.AluOpType
AF = mybir.ActivationFunctionType
AX = mybir.AxisListType

D = 2048
DC = 16
H = 16
HD = 128
SCALE = HD ** -0.5
EPS_ATTN = 1e-6
EPS_LN = 1e-5
EPS_TOPK = 1e-9
IC_FF = 8
NBUF = 3

ENGS = ("sync", "gpsimd", "vector", "scalar", "tensor")
PHASES = ("init", "ln_oth", "state", "ln_own", "attn", "router", "experts")


BLOB_ITEMS = [
    ("ident", (128, 128)),
    ("onec", (128, 1)),
    ("onerow", (1, 128)),
    ("masku", (128, 128)),
    ("masku4", (128, 512)),
    ("sel4", (4, 512)),
    ("ones4", (4, 4)),
    ("mn2", (4, 2048)),
    ("wrT", (128, 256)),
    ("mnT", (128, 16, 4)),
    ("memT", (16, 128, 512)),
    ("wqT", (16, 128, 2048)),
    ("wkT", (16, 128, 2048)),
    ("wvT", (16, 128, 2048)),
    ("woT", (16, 128, 2048)),
    ("swg", (8, 128, 2048)),
    ("swu", (8, 128, 2048)),
    ("swd", (16, 128, 1024)),
    ("wg", (16, 8, 128, 2048)),
    ("wu", (16, 8, 128, 2048)),
    ("wd", (16, 16, 128, 1024)),
]


def blob_offsets():
    offs, o = {}, 0
    for name, shape in BLOB_ITEMS:
        n = 1
        for s in shape:
            n *= s
        offs[name] = (o, shape)
        o += n
    return offs, o


BLOB_OFFS, BLOB_N = blob_offsets()


def boff(name, *idx):
    off, shape = BLOB_OFFS[name]
    rem = 1
    for s in shape[len(idx):]:
        rem *= s
    flat = 0
    for k in range(len(idx)):
        flat = flat * shape[k] + idx[k]
    return off + flat * rem


def _strips(W):
    K, N = W.shape
    Kc, Nc = K // 128, N // 128
    return np.ascontiguousarray(
        W.reshape(Kc, 128, Nc, 128).transpose(2, 1, 0, 3).reshape(Nc, 128, Kc * 128))


def _strips_small(W):
    K, N = W.shape
    Kc = K // 128
    return np.ascontiguousarray(W.reshape(Kc, 128, N).transpose(1, 0, 2).reshape(128, Kc * N))


def build_blob(i):
    f = np.float32
    parts = {}
    parts["ident"] = np.eye(128, dtype=f)
    parts["onec"] = np.ones((128, 1), dtype=f)
    parts["onerow"] = np.ones((1, 128), dtype=f)
    parts["masku"] = np.triu(np.ones((128, 128), dtype=f))
    parts["masku4"] = np.tile(parts["masku"], (1, 4))
    s4 = np.zeros((4, 512), dtype=f)
    for m in range(4):
        s4[m, m * 128:(m + 1) * 128] = 1.0
    parts["sel4"] = s4
    parts["ones4"] = np.ones((4, 4), dtype=f)
    mn = np.asarray(i["L0_attn_memory_norms"], dtype=f)
    parts["mn2"] = np.ascontiguousarray(mn.reshape(4, 2048))
    parts["wrT"] = _strips_small(np.ascontiguousarray(np.asarray(i["L0_ffn_Wr"], f).T))
    parts["mnT"] = np.ascontiguousarray(mn.transpose(2, 1, 0))
    mem = np.asarray(i["L0_attn_memories"], dtype=f)
    parts["memT"] = np.ascontiguousarray(mem.transpose(1, 2, 0, 3).reshape(16, 128, 512))
    for nm, key in (("wqT", "L0_attn_w_q"), ("wkT", "L0_attn_w_k"),
                    ("wvT", "L0_attn_w_v"), ("woT", "L0_attn_w_o")):
        parts[nm] = _strips(np.ascontiguousarray(np.asarray(i[key], f).T))
    parts["swg"] = _strips(np.asarray(i["L0_ffn_sWg"], f))
    parts["swu"] = _strips(np.asarray(i["L0_ffn_sWu"], f))
    parts["swd"] = _strips(np.asarray(i["L0_ffn_sWd"], f))
    parts["wg"] = np.stack([_strips(np.asarray(i["L0_ffn_Wg"], f)[e]) for e in range(16)])
    parts["wu"] = np.stack([_strips(np.asarray(i["L0_ffn_Wu"], f)[e]) for e in range(16)])
    parts["wd"] = np.stack([_strips(np.asarray(i["L0_ffn_Wd"], f)[e]) for e in range(16)])
    blob = np.empty(BLOB_N, dtype=f)
    for name, (off, shape) in BLOB_OFFS.items():
        a = parts[name]
        assert tuple(a.shape) == tuple(shape), (name, a.shape, shape)
        blob[off:off + a.size] = a.ravel()
    return blob



class Ch:
    __slots__ = ("sem", "n", "step")

    def __init__(self, sem, step=1):
        self.sem = sem
        self.n = 0
        self.step = step


class Builder:
    def __init__(self, nc):
        self.nc = nc
        self.spacer = None
        self.last_vsp = 0
        self.last_asp = 0
        self.s_V = None
        self.s_A = None
        self.ops = {e: [] for e in ENGS}

    @staticmethod
    def _tiny(ap):
        try:
            pat = ap.ap
            parts = pat[0][1]
            free = 1
            for sc in pat[1:]:
                free *= sc[1]
            return parts < 128 or free < 128
        except Exception:
            return True

    def _vspace(self, inc):
        sp = self.spacer
        ch, k = inc if inc is not None else (self.s_V, 0)
        self.emit("vector", lambda e, o=sp[0], i=sp[1]: e.tensor_copy(o, i),
                  [(self.s_V, self.last_vsp)], (self.s_V, k + 1))
        self.last_vsp = self.s_V.n

    def _aspace(self, inc):
        sp = self.aspacer
        ch, k = inc if inc is not None else (self.s_A, 0)
        self.emit("scalar", lambda e, o=sp[0], i=sp[1]: e.copy(o, i),
                  [(self.s_A, self.last_asp)], (self.s_A, k + 1))
        self.last_asp = self.s_A.n

    def emit(self, eng, fn, waits=(), inc=None):
        seen = {}
        for ch, v in waits:
            if v is None or v <= 0:
                continue
            assert v <= ch.n, f"wait target {v} > emitted {ch.n} ({eng})"
            if id(ch) in seen and seen[id(ch)][1] >= v:
                continue
            seen[id(ch)] = (ch, v)
        wl = list(seen.values())
        il = None
        if inc is not None:
            ch, k = inc
            ch.n += k
            il = (ch.sem, k)

        def g(E, wl=wl, il=il, fn=fn):
            for ch, v in wl:
                E.wait_ge(ch.sem, v)
            ins = fn(E)
            if il is not None:
                ins.then_inc(il[0], il[1])
        self.ops[eng].append(g)

    def mm(self, out, lhsT, rhs, start, stop, waits=(), inc=None):
        self.emit("tensor",
                  lambda t, o=out, l=lhsT, r=rhs, a=start, bb=stop:
                  t.matmul(o, l, r, start=a, stop=bb), waits, inc)

    def tr(self, out, in_, ident, waits=(), inc=None):
        self.emit("tensor",
                  lambda t, o=out, i=in_, d=ident: t.transpose(o, i, d), waits, inc)

    def tt(self, out, a, b_, op, waits=(), inc=None):
        t = self._tiny(out)
        self.emit("vector",
                  lambda e, o=out, x=a, y=b_, p=op: e.tensor_tensor(o, x, y, p),
                  waits, None if t else inc)
        if t:
            self._vspace(inc)

    def ts(self, out, a, s1, op0, s2=None, op1=None, waits=(), inc=None):
        t = self._tiny(out)
        if op1 is None:
            self.emit("vector", lambda e, o=out, x=a, s=s1, p=op0:
                      e.tensor_scalar(o, x, s, None, p), waits, None if t else inc)
        else:
            self.emit("vector", lambda e, o=out, x=a, sa=s1, sb=s2, p0=op0, p1=op1:
                      e.tensor_scalar(o, x, sa, sb, p0, p1), waits, None if t else inc)
        if t:
            self._vspace(inc)

    def stt(self, out, a, s, b_, op0, op1, waits=(), inc=None):
        t = self._tiny(out)
        self.emit("vector", lambda e, o=out, x=a, sc=s, y=b_, p0=op0, p1=op1:
                  e.scalar_tensor_tensor(o, x, sc, y, p0, p1), waits,
                  None if t else inc)
        if t:
            self._vspace(inc)

    def act(self, out, in_, func, bias=None, scale=None, waits=(), inc=None):
        t = self._tiny(out)

        def f(e, o=out, i=in_, fu=func, bi=bias, s=scale):
            kw = {}
            if bi is not None:
                kw["bias"] = bi
            if s is not None:
                kw["scale"] = s
            return e.activation(o, i, fu, **kw)
        self.emit("scalar", f, waits, None if t else inc)
        if t:
            self._aspace(inc)

    def red(self, out, in_, op, waits=(), inc=None):
        self.emit("vector", lambda e, o=out, i=in_, p=op:
                  e.tensor_reduce(o, i, AX.X, p), waits, None)
        self._vspace(inc)

    def recip(self, out, in_, waits=(), inc=None):
        t = self._tiny(out)
        self.emit("vector", lambda e, o=out, i=in_: e.reciprocal(o, i), waits,
                  None if t else inc)
        if t:
            self._vspace(inc)

    def cp(self, out, in_, waits=(), inc=None):
        t = self._tiny(out)
        self.emit("vector", lambda e, o=out, i=in_: e.tensor_copy(o, i), waits,
                  None if t else inc)
        if t:
            self._vspace(inc)

    def memset(self, ap, val, waits=(), inc=None):
        self.emit("vector", lambda e, a=ap, v=val: e.memset(a, v), waits, inc)

    def dma(self, eng, out, in_, waits=(), inc=None):
        self.emit(eng, lambda e, o=out, i=in_: e.dma_start(out=o, in_=i), waits, inc)




def build_nc(dump=None, upto=None):
    if upto == "attn1":
        run = PHASES[:PHASES.index("ln_own") + 1]
    else:
        run = PHASES if upto is None else PHASES[:PHASES.index(upto) + 1]
    nc = bass.Bass("TRN2", target_bir_lowering=False)
    xT = nc.dram_tensor("xT", [D, 2048], F32, kind="ExternalInput").ap()
    blob = nc.dram_tensor("blob", [BLOB_N], F32, kind="ExternalInput").ap()
    cpage_h = nc.dram_tensor("cpage", [128, 32], F32, kind="ExternalInput").ap()
    outT = nc.dram_tensor("outT", [D, 1024], F32, kind="ExternalOutput").ap()
    dbg = None
    if dump is not None:
        dbg = nc.dram_tensor("dbg", [128, 16384], F32, kind="ExternalOutput").ap()

    blob_t = blob.tensor

    AR_N = 8192
    xn = nc.alloc_sbuf_tensor("xn", [128, DC * 1024], F32).ap()
    acc = nc.alloc_sbuf_tensor("acc", [128, DC * 1024], F32).ap()
    ar = nc.alloc_sbuf_tensor("ar", [128, AR_N], F32).ap()
    ring = nc.alloc_sbuf_tensor("ring", [128, NBUF * 2048], F32).ap()
    S_sb = nc.alloc_sbuf_tensor("S_sb", [128, H * 128], F32).ap()
    Z_sb = nc.alloc_sbuf_tensor("Z_sb", [128, 16], F32).ap()
    mxn = nc.alloc_sbuf_tensor("mxn", [128, 32], F32).ap()
    mnT_sb = nc.alloc_sbuf_tensor("mnT_sb", [128, 64], F32).ap()
    r0 = nc.alloc_sbuf_tensor("r0", [1, 1536], F32).ap()
    sm = nc.alloc_sbuf_tensor("sm", [128, 32], F32).ap()
    wgtg = nc.alloc_sbuf_tensor("wgtg", [4, 16], F32).ap()
    idn = nc.alloc_sbuf_tensor("idn", [128, 128], F32).ap()
    msku4 = nc.alloc_sbuf_tensor("msku4", [128, 512], F32).ap()
    sel4 = nc.alloc_sbuf_tensor("sel4", [4, 512], F32).ap()
    cpg = nc.alloc_sbuf_tensor("cpg", [128, 32], F32).ap()
    sgr = nc.alloc_sbuf_tensor("sgr", [1, 48], F32).ap()
    sgb = nc.alloc_sbuf_tensor("sgb", [128, 16], F32).ap()
    on4 = nc.alloc_sbuf_tensor("on4", [4, 4], F32).ap()
    onr = nc.alloc_sbuf_tensor("onr", [1, 128], F32).ap()

    pb = [nc.alloc_psum_tensor(f"pb{i}", [128, 512], F32).ap() for i in range(8)]

    sems = {n: nc.alloc_semaphore(n) for n in
            ("s_w", "s_x", "s_o", "s_T", "s_V", "s_A")}
    s_w = Ch(sems["s_w"], 16)
    s_x = Ch(sems["s_x"], 16)
    s_o = Ch(sems["s_o"], 16)
    s_T = Ch(sems["s_T"])
    s_V = Ch(sems["s_V"])
    s_A = Ch(sems["s_A"])

    cwT = S_sb[0:16, 1024:2048]
    b = Builder(nc)
    b.spacer = (sm[:, 24:25], sm[:, 31:32])
    b.aspacer = (sm[:, 23:24], sm[:, 31:32])
    b.s_V = s_V
    b.s_A = s_A

    def bap(off, P, F):
        return bass.AP(blob_t, off, [[F, P], [1, F]])

    class WS:
        def __init__(self):
            self.cnt = 0
            self.freed = []

        def load(self, off, cols, rows=128):
            i = self.cnt
            slot = i % NBUF
            waits = []
            if i >= NBUF:
                waits.append((s_T, self.freed[i - NBUF]))
            b.dma("sync", ring[0:rows, slot * 2048: slot * 2048 + cols],
                  bap(off, rows, cols), waits=waits, inc=(s_w, 16))
            self.cnt += 1
            if i < NBUF:
                v = 16 * self.cnt
                b.emit("sync", lambda e, vv=v: e.wait_ge(s_w.sem, vv))
            return slot, 16 * self.cnt

        def done(self):
            self.freed.append(s_T.n)

    ws = WS()

    class XS:
        def __init__(self):
            self.cnt = 0
            self.freed = []

        def load(self, dst, src, depth=2, waits=()):
            i = self.cnt
            wl = list(waits)
            if i >= depth:
                wl += self.freed[i - depth]
            b.dma("gpsimd", dst, src, waits=wl, inc=(s_x, 16))
            self.cnt += 1
            if i < depth:
                v = s_x.n
                b.emit("gpsimd", lambda e, vv=v: e.wait_ge(s_x.sem, vv))

        def done(self, *chs):
            self.freed.append([(c, c.n) for c in chs])

    LN = dict(XS0=0, XS1=1024, SQA=2048, AB=3072, BB=4096,
              R1=5120, R2=6144, R3=7168)
    P2 = dict(KPT=0, KTK=1024, VT=2048, VTK=3072, S1=4096, S2=4608)
    AT = dict(ATTNB=0, SQ=4096, KPT=4608, KTK=5120, VTK=5632,
              S1=6144, S2=6656, SMT=7168, RBM=7296)
    RT = dict(LGT=0, PRB=1024, T0=1040, T1=1056, SCR=1072)
    FF = dict(SU=0)
    R0 = dict(CROW=0, DR=1024)

    one_col = sm[:, 31:32]
    mcol = sm[:, 30:31]
    em4 = sm[:, 28:29]
    tmpc = sm[:, 27:28]
    mq_sb = sm[:, 26:27]
    c1s = sm[:, 25:26]
    m1s = sm[:, 21:22]
    m2s = sm[:, 22:23]
    naec = sm[0:4, 15:16]
    brwc = sm[0:4, 16:17]
    rlc = sm[0:4, 11:12]
    rlc2 = sm[0:4, 12:13]
    scol = sm[0:4, 13:14]
    sc2 = sm[0:4, 14:15]

    def init_phase():
        b.dma("gpsimd", sm[:, 31:32], bap(boff("onec"), 128, 1), inc=(s_x, 16))
        b.dma("gpsimd", onr[0:1, :], bap(boff("onerow"), 1, 128), inc=(s_x, 16))
        b.dma("gpsimd", cpg[:, :], cpage_h[:, :], inc=(s_x, 16))
        b.dma("gpsimd", idn[:, :], bap(boff("ident"), 128, 128), inc=(s_x, 16))
        b.dma("gpsimd", msku4[:, :], bap(boff("masku4"), 128, 512), inc=(s_x, 16))
        b.dma("gpsimd", sel4[:, :], bap(boff("sel4"), 4, 512), inc=(s_x, 16))
        b.dma("gpsimd", on4[:, :], bap(boff("ones4"), 4, 4), inc=(s_x, 16))
        b.dma("gpsimd", mnT_sb[:, :], bap(boff("mnT"), 128, 64), inc=(s_x, 16))
        b.dma("gpsimd", ar[0:4, 0:2048], bap(boff("mn2"), 4, 2048), inc=(s_x, 16))
        xv0 = s_x.n
        b.emit("gpsimd", lambda e, vv=xv0: e.wait_ge(s_x.sem, vv))
        b.act(sgr[0:1, 0:16], cpg[0:1, 2:18], AF.Sigmoid,
              waits=[(s_x, xv0)], inc=(s_A, 1))
        b.act(sgr[0:1, 16:32], sgr[0:1, 0:16], AF.Copy, bias=1.0, scale=-1.0,
              waits=[(s_A, s_A.n)], inc=(s_A, 1))
        b.mm(pb[4][:, 0:2], onr[0:1, :], cpg[0:1, 20:22], True, True,
             waits=[(s_x, xv0), (s_V, s_V.n)], inc=(s_T, 1))
        b.cp(mcol, pb[4][:, 0:1], waits=[(s_T, s_T.n)], inc=(s_V, 1))
        b.mm(pb[5][:, 0:16], onr[0:1, :], sgr[0:1, 0:16], True, True,
             waits=[(s_A, s_A.n), (s_V, s_V.n)], inc=(s_T, 1))
        b.cp(sgb[:, :], pb[5][:, 0:16], waits=[(s_T, s_T.n)], inc=(s_V, 1))
        b.red(em4[0:4, 0:1], ar[0:4, 0:2048], ALU.add,
              waits=[(s_x, xv0)], inc=(s_V, 1))
        b.ts(em4[0:4, 0:1], em4[0:4, 0:1], 1e-6, ALU.is_lt,
             waits=[(s_V, s_V.n)], inc=(s_V, 1))
        b.mm(pb[3][0:4, 0:2], on4[0:4, 0:4], sm[0:4, 28:30], True, True,
             waits=[(s_V, s_V.n), (s_x, xv0)], inc=(s_T, 1))
        b.ts(naec, pb[3][0:4, 0:1], 3.5, ALU.is_gt,
             waits=[(s_T, s_T.n)], inc=(s_V, 1))
        b.ts(naec, naec, -1.0, ALU.mult, inc=(s_V, 1))
        b.ts(naec, naec, 1.0, ALU.add, inc=(s_V, 1))
        b.ts(brwc, em4[0:4, 0:1], -1e30, ALU.mult, inc=(s_V, 1))

    def ln_phase(col0, own):
        xs = XS()
        for dc in range(DC):
            slot = LN["XS0"] if dc % 2 == 0 else LN["XS1"]
            xt = ar[:, slot:slot + 1024]
            wl = [(s_V, s_V.n), (s_T, s_T.n)] if dc < 2 else ()
            xs.load(xt, bass.AP(xT.tensor, dc * 128 * 2048 + col0,
                                [[2048, 128], [1, 1024]]), waits=wl)
            sq = ar[:, LN["SQA"]:LN["SQA"] + 1024]
            b.act(sq, xt, AF.Square, waits=[(s_x, s_x.n), (s_T, s_T.n)],
                  inc=(s_A, 1))
            for g in range(2):
                b.mm(pb[g][0:1, :], one_col, xt[:, g * 512:(g + 1) * 512],
                     dc == 0, dc == DC - 1,
                     waits=[(s_x, s_x.n)] if g == 0 else (),
                     inc=(s_T, 1) if g == 1 else None)
            for g in range(2):
                b.mm(pb[2 + g][0:1, :], one_col, sq[:, g * 512:(g + 1) * 512],
                     dc == 0, dc == DC - 1,
                     waits=[(s_A, s_A.n)] if g == 0 else (),
                     inc=(s_T, 1) if g == 1 else None)
            xs.done(s_T)
        R1 = ar[0:1, LN["R1"]:LN["R1"] + 1024]
        R2 = ar[0:1, LN["R2"]:LN["R2"] + 1024]
        R3 = ar[0:1, LN["R3"]:LN["R3"] + 1024]
        for g in range(2):
            b.ts(R1[0:1, g * 512:(g + 1) * 512], pb[g][0:1, :], 1.0 / D,
                 ALU.mult, waits=[(s_T, s_T.n)], inc=(s_V, 1))
            b.ts(R2[0:1, g * 512:(g + 1) * 512], pb[2 + g][0:1, :], 1.0 / D,
                 ALU.mult, inc=(s_V, 1))
        b.tt(R3, R1, R1, ALU.mult, inc=(s_V, 1))
        b.tt(R2, R2, R3, ALU.subtract, inc=(s_V, 1))
        b.ts(R2, R2, EPS_LN, ALU.add, inc=(s_V, 1))
        b.recip(R3, R2, inc=(s_V, 1))
        b.act(R2, R3, AF.Sqrt, waits=[(s_V, s_V.n)], inc=(s_A, 1))
        b.stt(R3, R1, -1.0, R2, ALU.mult, ALU.mult,
              waits=[(s_A, s_A.n)], inc=(s_V, 1))
        for half in range(2):
            b.mm(pb[4 + half][:, :], onr[0:1, :],
                 R2[0:1, half * 512:(half + 1) * 512], True, True,
                 waits=[(s_A, s_A.n), (s_V, s_V.n)], inc=(s_T, 1))
            b.cp(ar[:, LN["AB"] + half * 512:LN["AB"] + (half + 1) * 512],
                 pb[4 + half][:, :], waits=[(s_T, s_T.n)], inc=(s_V, 1))
            b.mm(pb[6 + half][:, :], onr[0:1, :],
                 R3[0:1, half * 512:(half + 1) * 512], True, True,
                 waits=[(s_V, s_V.n)], inc=(s_T, 1))
            b.cp(ar[:, LN["BB"] + half * 512:LN["BB"] + (half + 1) * 512],
                 pb[6 + half][:, :], waits=[(s_T, s_T.n)], inc=(s_V, 1))
        a_b = ar[:, LN["AB"]:LN["AB"] + 1024]
        b_b = ar[:, LN["BB"]:LN["BB"] + 1024]
        xs2 = XS()
        for dc in range(DC):
            if own:
                dst = acc[:, dc * 1024:(dc + 1) * 1024]
            else:
                base = LN["XS0"] if dc % 2 == 0 else LN["XS1"]
                dst = ar[:, base:base + 1024]
            wl2 = [(s_T, s_T.n), (s_A, s_A.n)] if (dc < 2 and not own) else ()
            xs2.load(dst, bass.AP(xT.tensor, dc * 128 * 2048 + col0,
                                  [[2048, 128], [1, 1024]]), waits=wl2)
            xnt = xn[:, dc * 1024:(dc + 1) * 1024]
            b.tt(xnt, dst, a_b, ALU.mult, waits=[(s_x, s_x.n)], inc=(s_V, 1))
            b.tt(xnt, xnt, b_b, ALU.add, inc=(s_V, 1))
            b.red(tmpc, xnt, ALU.add, inc=(s_V, 1))
            if own:
                b.tt(mxn[:, dc:dc + 1], mxn[:, dc:dc + 1], tmpc, ALU.add,
                     inc=(s_V, 1))
            else:
                b.cp(mxn[:, dc:dc + 1], tmpc, inc=(s_V, 1))
            xs2.done(s_V)

    def sigma(dst, src_psum, s1c, s2c, n):
        t1 = ar[:, s1c:s1c + n]
        t2 = ar[:, s2c:s2c + n]
        b.ts(t1, src_psum, 0.0, ALU.min, waits=[(s_T, s_T.n)], inc=(s_V, 1))
        b.act(t2, t1, AF.Exp, waits=[(s_V, s_V.n)], inc=(s_A, 1))
        b.stt(dst, src_psum, 0.0, t2, ALU.max, ALU.add,
              waits=[(s_A, s_A.n)], inc=(s_V, 1))

    def state_build():
        for h in range(H):
            slot, wv = ws.load(boff("wkT", h), 2048)
            base = slot * 2048
            for tg in range(2):
                for dc in range(DC):
                    b.mm(pb[4][:, :], ring[:, base + dc * 128:base + (dc + 1) * 128],
                         xn[:, dc * 1024 + tg * 512: dc * 1024 + (tg + 1) * 512],
                         dc == 0, dc == DC - 1,
                         waits=[(s_w, wv), (s_V, s_V.n)] if dc == 0 else (),
                         inc=(s_T, 1) if dc == DC - 1 else None)
                sigma(ar[:, P2["KPT"] + tg * 512: P2["KPT"] + (tg + 1) * 512],
                      pb[4][:, :], P2["S1"], P2["S2"], 512)
            ws.done()
            kpT = ar[:, P2["KPT"]:P2["KPT"] + 1024]
            b.red(tmpc, kpT, ALU.add, inc=(s_V, 1))
            b.ts(Z_sb[:, h:h + 1], tmpc, mcol, ALU.mult, inc=(s_V, 1))
            for q in range(2):
                for c in range(4):
                    cc = q * 4 + c
                    b.tr(pb[6][:, c * 128:(c + 1) * 128],
                         kpT[:, cc * 128:(cc + 1) * 128], idn[:, :],
                         waits=[(s_V, s_V.n)] if c == 0 else (),
                         inc=(s_T, 1) if c == 3 else None)
                for c in range(4):
                    cc = q * 4 + c
                    b.cp(ar[:, P2["KTK"] + cc * 128:P2["KTK"] + (cc + 1) * 128],
                         pb[6][:, c * 128:(c + 1) * 128],
                         waits=[(s_T, s_T.n)] if c == 0 else (), inc=(s_V, 1))
            slot, wv2 = ws.load(boff("wvT", h), 2048)
            base = slot * 2048
            for tg in range(2):
                for dc in range(DC):
                    b.mm(pb[5][:, :], ring[:, base + dc * 128:base + (dc + 1) * 128],
                         xn[:, dc * 1024 + tg * 512: dc * 1024 + (tg + 1) * 512],
                         dc == 0, dc == DC - 1,
                         waits=[(s_w, wv2), (s_V, s_V.n)] if dc == 0 else (),
                         inc=(s_T, 1) if dc == DC - 1 else None)
                b.cp(ar[:, P2["VT"] + tg * 512:P2["VT"] + (tg + 1) * 512],
                     pb[5][:, :], waits=[(s_T, s_T.n)], inc=(s_V, 1))
            ws.done()
            vT = ar[:, P2["VT"]:P2["VT"] + 1024]
            for q in range(2):
                for c in range(4):
                    cc = q * 4 + c
                    b.tr(pb[6][:, c * 128:(c + 1) * 128],
                         vT[:, cc * 128:(cc + 1) * 128], idn[:, :],
                         waits=[(s_V, s_V.n)] if c == 0 else (),
                         inc=(s_T, 1) if c == 3 else None)
                for c in range(4):
                    cc = q * 4 + c
                    b.cp(ar[:, P2["VTK"] + cc * 128:P2["VTK"] + (cc + 1) * 128],
                         pb[6][:, c * 128:(c + 1) * 128],
                         waits=[(s_T, s_T.n)] if c == 0 else (), inc=(s_V, 1))
            for c in range(8):
                b.mm(pb[7][:, 0:128],
                     ar[:, P2["KTK"] + c * 128:P2["KTK"] + (c + 1) * 128],
                     ar[:, P2["VTK"] + c * 128:P2["VTK"] + (c + 1) * 128],
                     c == 0, c == 7,
                     waits=[(s_V, s_V.n)] if c == 0 else (),
                     inc=(s_T, 1) if c == 7 else None)
            b.ts(S_sb[:, h * 128:(h + 1) * 128], pb[7][:, 0:128], mcol,
                 ALU.mult, waits=[(s_T, s_T.n)], inc=(s_V, 1))

    def attention(nth=2, nhp=2):
        for th in range(nth):
            tb = 512 * th
            for hp in range(nhp):
                for hh in range(8):
                    h = hp * 8 + hh
                    sq_ap = ar[:, AT["SQ"]:AT["SQ"] + 512]
                    kpT = ar[:, AT["KPT"]:AT["KPT"] + 512]
                    ktk = ar[:, AT["KTK"]:AT["KTK"] + 512]
                    vtk = ar[:, AT["VTK"]:AT["VTK"] + 512]
                    slot, wv = ws.load(boff("wqT", h), 2048)
                    base = slot * 2048
                    for dc in range(DC):
                        b.mm(pb[0][:, :], ring[:, base + dc * 128:base + (dc + 1) * 128],
                             xn[:, dc * 1024 + tb: dc * 1024 + tb + 512],
                             dc == 0, dc == DC - 1,
                             waits=[(s_w, wv), (s_V, s_V.n)] if dc == 0 else (),
                             inc=(s_T, 1) if dc == DC - 1 else None)
                    if th == 0:
                        for dc in range(DC):
                            b.mm(pb[3][:, 0:2],
                                 ring[:, base + dc * 128:base + (dc + 1) * 128],
                                 mxn[:, dc:dc + 2], dc == 0, dc == DC - 1,
                                 inc=(s_T, 1) if dc == DC - 1 else None)
                    sigma(sq_ap, pb[0][:, :], AT["S1"], AT["S2"], 512)
                    ws.done()
                    if th == 0:
                        b.cp(mq_sb, pb[3][:, 0:1], waits=[(s_T, s_T.n)],
                             inc=(s_V, 1))
                        b.mm(pb[3][0:4, 4:6], mnT_sb[:, h * 4:(h + 1) * 4],
                             sm[:, 26:28], True, True, waits=[(s_V, s_V.n)],
                             inc=(s_T, 1))
                        b.ts(rlc, pb[3][0:4, 4:5], SCALE / 2048.0, ALU.mult,
                             waits=[(s_T, s_T.n)], inc=(s_V, 1))
                        b.tt(rlc, rlc, brwc, ALU.add, inc=(s_V, 1))
                        b.act(rlc2, rlc, AF.Exp, waits=[(s_V, s_V.n)],
                              inc=(s_A, 1))
                        b.mm(pb[3][0:4, 6:8], on4[0:4, 0:4], sm[0:4, 12:14],
                             True, True, waits=[(s_A, s_A.n)], inc=(s_T, 1))
                        b.ts(scol, pb[3][0:4, 6:7], 1e-30, ALU.add,
                             waits=[(s_T, s_T.n)], inc=(s_V, 1))
                        b.recip(sc2, scol, inc=(s_V, 1))
                        b.tt(wgtg[0:4, h:h + 1], rlc2, sc2, ALU.mult,
                             inc=(s_V, 1))
                        b.tt(wgtg[0:4, h:h + 1], wgtg[0:4, h:h + 1],
                             sgb[0:4, h:h + 1], ALU.mult, inc=(s_V, 1))
                        b.tt(wgtg[0:4, h:h + 1], wgtg[0:4, h:h + 1],
                             naec, ALU.mult, inc=(s_V, 1))
                    slot, wv = ws.load(boff("wkT", h), 2048)
                    base = slot * 2048
                    for dc in range(DC):
                        b.mm(pb[0][:, :], ring[:, base + dc * 128:base + (dc + 1) * 128],
                             xn[:, dc * 1024 + tb: dc * 1024 + tb + 512],
                             dc == 0, dc == DC - 1,
                             waits=[(s_w, wv), (s_V, s_V.n)] if dc == 0 else (),
                             inc=(s_T, 1) if dc == DC - 1 else None)
                    sigma(kpT, pb[0][:, :], AT["S1"], AT["S2"], 512)
                    ws.done()
                    for c in range(4):
                        b.tr(pb[4][:, c * 128:(c + 1) * 128],
                             kpT[:, c * 128:(c + 1) * 128], idn[:, :],
                             waits=[(s_V, s_V.n)] if c == 0 else (),
                             inc=(s_T, 1) if c == 3 else None)
                    for c in range(4):
                        b.cp(ktk[:, c * 128:(c + 1) * 128],
                             pb[4][:, c * 128:(c + 1) * 128],
                             waits=[(s_T, s_T.n)] if c == 0 else (), inc=(s_V, 1))
                    slot, wv = ws.load(boff("wvT", h), 2048)
                    base = slot * 2048
                    for c in range(4):
                        bank = pb[1] if c % 2 == 0 else pb[2]
                        for dc in range(DC):
                            b.mm(bank[:, 0:128],
                                 xn[:, dc * 1024 + tb + c * 128:
                                    dc * 1024 + tb + (c + 1) * 128],
                                 ring[:, base + dc * 128:base + (dc + 1) * 128],
                                 dc == 0, dc == DC - 1,
                                 waits=[(s_w, wv), (s_V, s_V.n)] if dc == 0 else (),
                                 inc=(s_T, 1) if dc == DC - 1 else None)
                        b.cp(vtk[:, c * 128:(c + 1) * 128], bank[:, 0:128],
                             waits=[(s_T, s_T.n)], inc=(s_V, 1))
                    ws.done()
                    SMT5 = AT["SMT"]
                    SV = AT["SMT"] + 512
                    zb = sm[:, 4:8]
                    zc = sm[:, 17:21]
                    for c in range(4):
                        b.mm(pb[6][:, c * 128:(c + 1) * 128],
                             ktk[:, c * 128:(c + 1) * 128],
                             vtk[:, c * 128:(c + 1) * 128], True, True,
                             waits=[(s_V, s_V.n)] if c == 0 else (),
                             inc=(s_T, 1) if c == 3 else None)
                    for c in range(4):
                        b.mm(pb[4][:, c * 128:(c + 1) * 128],
                             kpT[:, c * 128:(c + 1) * 128],
                             sq_ap[:, c * 128:(c + 1) * 128], True, True,
                             inc=(s_T, 1) if c == 3 else None)
                    smt5 = ar[:, SMT5:SMT5 + 512]
                    b.tt(smt5, pb[4][:, 0:512], msku4[:, :], ALU.mult,
                         waits=[(s_T, s_T.n)], inc=(s_V, 1))
                    kp3d = bass.AP(ar.tensor, AT["KPT"],
                                   [[AR_N, 128], [128, 4], [1, 128]])
                    b.red(zb, kp3d, ALU.add, inc=(s_V, 1))
                    b.cp(zc[:, 0:1], Z_sb[:, h:h + 1], inc=(s_V, 1))
                    for c in range(3):
                        b.tt(zc[:, c + 1:c + 2], zc[:, c:c + 1], zb[:, c:c + 1],
                             ALU.add, inc=(s_V, 1))
                    b.tt(Z_sb[:, h:h + 1], zc[:, 3:4], zb[:, 3:4], ALU.add,
                         inc=(s_V, 1))
                    sv = [S_sb[:, h * 128:(h + 1) * 128]]
                    for c in range(3):
                        dst = ar[:, SV + c * 128:SV + (c + 1) * 128]
                        b.tt(dst, sv[-1], pb[6][:, c * 128:(c + 1) * 128],
                             ALU.add, inc=(s_V, 1))
                        sv.append(dst)
                    sv4 = ar[:, SV + 384:SV + 512]
                    b.tt(sv4, sv[3], pb[6][:, 384:512], ALU.add, inc=(s_V, 1))
                    slot, wv = ws.load(boff("memT", h), 512)
                    base = slot * 2048
                    b.mm(pb[5][0:4, :], mnT_sb[:, h * 4:(h + 1) * 4], sq_ap,
                         True, True, waits=[(s_V, s_V.n)], inc=(s_T, 1))
                    coef = ar[0:4, AT["S1"]:AT["S1"] + 512]
                    b.ts(coef, pb[5][0:4, :], EPS_ATTN, ALU.max,
                         waits=[(s_T, s_T.n)], inc=(s_V, 1))
                    b.recip(coef, coef, inc=(s_V, 1))
                    b.ts(coef, coef, wgtg[0:4, h:h + 1], ALU.mult, inc=(s_V, 1))
                    for m in range(4):
                        b.mm(pb[6][:, :], sel4[0:4, m * 128:(m + 1) * 128],
                             coef, True, True,
                             waits=[(s_V, s_V.n)], inc=(s_T, 1))
                        sqs = ar[:, AT["S2"]:AT["S2"] + 512]
                        b.tt(sqs, sq_ap, pb[6][:, :], ALU.mult,
                             waits=[(s_T, s_T.n)], inc=(s_V, 1))
                        b.mm(pb[5][:, :], ring[:, base + m * 128:base + (m + 1) * 128],
                             sqs, m == 0, m == 3,
                             waits=[(s_w, wv), (s_V, s_V.n)], inc=(s_T, 1))
                    ws.done()
                    vready = s_V.n
                    b.mm(pb[7][0:1, 0:512], one_col, smt5, True, False,
                         waits=[(s_V, vready)])
                    for c in range(4):
                        b.mm(pb[7][0:1, c * 128:(c + 1) * 128], zc[:, c:c + 1],
                             sq_ap[:, c * 128:(c + 1) * 128], False, c == 3,
                             inc=(s_T, 1) if c == 3 else None)
                    nbk = [pb[0], pb[1], pb[2], pb[3]]
                    for c in range(4):
                        b.mm(nbk[c][:, 0:128], vtk[:, c * 128:(c + 1) * 128],
                             smt5[:, c * 128:(c + 1) * 128], True, False)
                        b.mm(nbk[c][:, 0:128], sv[c],
                             sq_ap[:, c * 128:(c + 1) * 128], False, True,
                             inc=(s_T, 1))
                    b.cp(S_sb[:, h * 128:(h + 1) * 128], sv4,
                         waits=[(s_T, s_T.n)], inc=(s_V, 1))
                    dr5 = r0[0:1, R0["DR"]:R0["DR"] + 512]
                    b.ts(dr5, pb[7][0:1, 0:512], EPS_ATTN, ALU.max,
                         waits=[(s_T, s_T.n)], inc=(s_V, 1))
                    b.recip(dr5, dr5, inc=(s_V, 1))
                    b.ts(dr5, dr5, sgr[0:1, 16 + h:17 + h], ALU.mult,
                         inc=(s_V, 1))
                    b.mm(pb[7][:, 0:512], onr[0:1, :], dr5, True, True,
                         waits=[(s_V, s_V.n)], inc=(s_T, 1))
                    rbm5 = ar[:, SMT5:SMT5 + 512]
                    b.cp(rbm5, pb[7][:, 0:512], waits=[(s_T, s_T.n)],
                         inc=(s_V, 1))
                    for c in range(4):
                        ab = ar[:, AT["ATTNB"] + hh * 512 + c * 128:
                                AT["ATTNB"] + hh * 512 + (c + 1) * 128]
                        b.tt(ab, nbk[c][:, 0:128],
                             rbm5[:, c * 128:(c + 1) * 128], ALU.mult,
                             inc=(s_V, 1))
                    abh = ar[:, AT["ATTNB"] + hh * 512:AT["ATTNB"] + (hh + 1) * 512]
                    b.tt(abh, abh, pb[5][:, :], ALU.add, waits=[(s_T, s_T.n)],
                         inc=(s_V, 1))
                for ec in range(DC):
                    slot, wv = ws.load(boff("woT", ec), 2048)
                    base = slot * 2048
                    bank = pb[0] if ec % 2 == 0 else pb[1]
                    for hh in range(8):
                        hc = hp * 8 + hh
                        b.mm(bank[:, :], ring[:, base + hc * 128:base + (hc + 1) * 128],
                             ar[:, AT["ATTNB"] + hh * 512:AT["ATTNB"] + (hh + 1) * 512],
                             hh == 0, hh == 7,
                             waits=[(s_w, wv), (s_V, s_V.n)] if hh == 0 else (),
                             inc=(s_T, 1) if hh == 7 else None)
                    ws.done()
                    at = acc[:, ec * 1024 + tb: ec * 1024 + tb + 512]
                    b.tt(at, at, bank[:, :], ALU.add, waits=[(s_T, s_T.n)],
                         inc=(s_V, 1))

    def router():
        slot, wv = ws.load(boff("wrT"), 256)
        base = slot * 2048
        for tg in range(2):
            for dc in range(DC):
                b.mm(pb[0][0:16, :], ring[:, base + dc * 16:base + (dc + 1) * 16],
                     xn[:, dc * 1024 + tg * 512: dc * 1024 + (tg + 1) * 512],
                     dc == 0, dc == DC - 1,
                     waits=[(s_w, wv), (s_V, s_V.n)] if dc == 0 else (),
                     inc=(s_T, 1) if dc == DC - 1 else None)
            b.ts(ar[0:16, RT["LGT"] + tg * 512:RT["LGT"] + (tg + 1) * 512],
                 pb[0][0:16, :], cpg[0:16, 0:1], ALU.add,
                 waits=[(s_T, s_T.n)], inc=(s_V, 1))
        ws.done()
        for i in range(8):
            lg = ar[0:16, RT["LGT"] + i * 128:RT["LGT"] + (i + 1) * 128]
            b.tr(pb[4][:, 0:16], lg, idn[0:16, 0:16], waits=[(s_V, s_V.n)],
                 inc=(s_T, 1))
            P = ar[:, RT["PRB"]:RT["PRB"] + 16]
            t0 = ar[:, RT["T0"]:RT["T0"] + 16]
            t1 = ar[:, RT["T1"]:RT["T1"] + 16]
            b.cp(P, pb[4][:, 0:16], waits=[(s_T, s_T.n)], inc=(s_V, 1))
            b.red(c1s, P, ALU.max, inc=(s_V, 1))
            b.ts(c1s, c1s, -1.0, ALU.mult, inc=(s_V, 1))
            b.act(P, P, AF.Exp, bias=c1s, waits=[(s_V, s_V.n)], inc=(s_A, 1))
            b.red(c1s, P, ALU.add, waits=[(s_A, s_A.n)], inc=(s_V, 1))
            b.recip(c1s, c1s, inc=(s_V, 1))
            b.ts(P, P, c1s, ALU.mult, inc=(s_V, 1))

            def first_mask(src, mcol_, out_mask):
                b.ts(t0, src, mcol_, ALU.is_equal, inc=(s_V, 1))
                cur, oth = t0, t1
                for k in (1, 2, 4, 8):
                    b.cp(oth[:, 0:k], cur[:, 0:k], inc=(s_V, 1))
                    b.tt(oth[:, k:16], cur[:, k:16], cur[:, 0:16 - k], ALU.add,
                         inc=(s_V, 1))
                    cur, oth = oth, cur
                b.ts(oth, src, mcol_, ALU.is_equal, inc=(s_V, 1))
                b.tt(cur, cur, oth, ALU.subtract, inc=(s_V, 1))
                b.ts(cur, cur, 0.0, ALU.is_equal, inc=(s_V, 1))
                b.tt(out_mask, cur, oth, ALU.mult, inc=(s_V, 1))

            fm1 = ar[:, RT["SCR"]:RT["SCR"] + 16]
            fm2 = ar[:, RT["SCR"] + 16:RT["SCR"] + 32]
            p2_ = ar[:, RT["SCR"] + 32:RT["SCR"] + 48]
            b.red(m1s, P, ALU.max, inc=(s_V, 1))
            first_mask(P, m1s, fm1)
            b.stt(p2_, fm1, -1e30, P, ALU.mult, ALU.add, inc=(s_V, 1))
            b.red(m2s, p2_, ALU.max, inc=(s_V, 1))
            first_mask(p2_, m2s, fm2)
            b.tt(fm1, fm1, fm2, ALU.add, inc=(s_V, 1))
            b.tt(m1s, m1s, m2s, ALU.add, inc=(s_V, 1))
            b.ts(m1s, m1s, EPS_TOPK, ALU.add, inc=(s_V, 1))
            b.recip(m1s, m1s, inc=(s_V, 1))
            b.tt(fm1, fm1, P, ALU.mult, inc=(s_V, 1))
            b.ts(fm1, fm1, m1s, ALU.mult, inc=(s_V, 1))
            b.tr(pb[4][0:16, 16:144], fm1, idn[:, :], waits=[(s_V, s_V.n)],
                 inc=(s_T, 1))
            b.cp(cwT[0:16, i * 128:(i + 1) * 128], pb[4][0:16, 16:144],
                 waits=[(s_T, s_T.n)], inc=(s_V, 1))

    def experts():
        bfree = {i: [(s_V, s_V.n), (s_A, s_A.n)] for i in range(8)}
        for e in range(-1, 16):
            routed = e >= 0
            if routed:
                crow = r0[0:1, R0["CROW"]:R0["CROW"] + 1024]
                b.dma("gpsimd", crow, cwT[e:e + 1, 0:1024],
                      waits=[(s_V, s_V.n)], inc=(s_x, 16))
                cxv = s_x.n
                for tg in range(2):
                    b.mm(pb[6 + tg][:, :], onr[0:1, :],
                         crow[0:1, tg * 512:(tg + 1) * 512], True, True,
                         waits=[(s_x, cxv)] + bfree[6 + tg], inc=(s_T, 1))
                    b.cp(S_sb[:, tg * 512:(tg + 1) * 512], pb[6 + tg][:, :],
                         waits=[(s_T, s_T.n)], inc=(s_V, 1))
                    bfree[6 + tg] = [(s_V, s_V.n)]
            for ic in range(IC_FF):
                if routed:
                    sg_, wvg = ws.load(boff("wg", e, ic), 2048)
                    su_, wvu = ws.load(boff("wu", e, ic), 2048)
                else:
                    sg_, wvg = ws.load(boff("swg", ic), 2048)
                    su_, wvu = ws.load(boff("swu", ic), 2048)
                bg, bu = sg_ * 2048, su_ * 2048
                gb = 0 if ic % 2 == 0 else 4
                ub = 2 if ic % 2 == 0 else 6
                for tg in range(2):
                    for dc in range(DC):
                        b.mm(pb[gb + tg][:, :], ring[:, bg + dc * 128:bg + (dc + 1) * 128],
                             xn[:, dc * 1024 + tg * 512: dc * 1024 + (tg + 1) * 512],
                             dc == 0, dc == DC - 1,
                             waits=([(s_w, wvg)] + bfree[gb + tg]) if dc == 0 else (),
                             inc=(s_T, 1) if dc == DC - 1 else None)
                gv = s_T.n
                for tg in range(2):
                    for dc in range(DC):
                        b.mm(pb[ub + tg][:, :], ring[:, bu + dc * 128:bu + (dc + 1) * 128],
                             xn[:, dc * 1024 + tg * 512: dc * 1024 + (tg + 1) * 512],
                             dc == 0, dc == DC - 1,
                             waits=([(s_w, wvu)] + bfree[ub + tg]) if dc == 0 else (),
                             inc=(s_T, 1) if dc == DC - 1 else None)
                uv = s_T.n
                for tg in range(2):
                    sut = ar[:, FF["SU"] + ic * 1024 + tg * 512:
                             FF["SU"] + ic * 1024 + (tg + 1) * 512]
                    b.act(sut, pb[gb + tg][:, :], AF.Silu, waits=[(s_T, gv)],
                          inc=(s_A, 1))
                    bfree[gb + tg] = [(s_A, s_A.n)]
                    b.tt(sut, sut, pb[ub + tg][:, :], ALU.mult,
                         waits=[(s_A, s_A.n), (s_T, uv)], inc=(s_V, 1))
                    bfree[ub + tg] = [(s_V, s_V.n)]
                    if routed:
                        b.tt(sut, sut, S_sb[:, tg * 512:(tg + 1) * 512],
                             ALU.mult, inc=(s_V, 1))
                ws.done()
                ws.done()
            for hc in range(DC):
                if routed:
                    sd, wvd = ws.load(boff("wd", e, hc), 1024)
                else:
                    sd, wvd = ws.load(boff("swd", hc), 1024)
                bd = sd * 2048
                for tg in range(2):
                    bi = (hc * 2 + tg) % 4
                    bank = pb[bi]
                    for ic in range(IC_FF):
                        b.mm(bank[:, :], ring[:, bd + ic * 128:bd + (ic + 1) * 128],
                             ar[:, FF["SU"] + ic * 1024 + tg * 512:
                                FF["SU"] + ic * 1024 + (tg + 1) * 512],
                             ic == 0, ic == IC_FF - 1,
                             waits=([(s_w, wvd)] + bfree[bi]) if ic == 0 else (),
                             inc=(s_T, 1) if ic == IC_FF - 1 else None)
                    at = acc[:, hc * 1024 + tg * 512: hc * 1024 + (tg + 1) * 512]
                    b.tt(at, at, bank[:, :], ALU.add, waits=[(s_T, s_T.n)],
                         inc=(s_V, 1))
                    bfree[bi] = [(s_V, s_V.n)]
                ws.done()

    init_phase()
    if "ln_oth" in run:
        ln_phase(0, own=False)
    if "state" in run:
        state_build()
    if "ln_own" in run:
        ln_phase(1024, own=True)
    if upto == "attn1":
        attention(nth=1, nhp=1)
    elif "attn" in run:
        attention()
    if "router" in run:
        router()
    if "experts" in run:
        experts()

    def dump_all(ap, cols):
        b.dma("gpsimd", dbg[:, 0:cols], ap,
              waits=[(s_V, s_V.n), (s_A, s_A.n), (s_T, s_T.n)], inc=(s_o, 16))

    if dump == "xn":
        dump_all(xn[:, :], 16384)
    elif dump == "acc":
        dump_all(acc[:, :], 16384)
    elif dump == "state":
        dump_all(S_sb[:, :], 2048)
        b.dma("gpsimd", dbg[:, 2048:2064], Z_sb[:, :], inc=(s_o, 16))
        b.dma("gpsimd", dbg[:, 2064:2080], mxn[:, :], inc=(s_o, 16))
        b.dma("gpsimd", dbg[:, 2080:2112], sm[:, :], inc=(s_o, 16))
    elif dump == "cw":
        b.dma("gpsimd", dbg[0:16, 0:1024], cwT[:, :],
              waits=[(s_V, s_V.n), (s_T, s_T.n)], inc=(s_o, 16))
        b.dma("gpsimd", dbg[0:4, 2048:2064], wgtg[:, :], inc=(s_o, 16))
        b.dma("gpsimd", dbg[0:1, 2064:2112], sgr[:, :], inc=(s_o, 16))
        b.dma("gpsimd", dbg[:, 2112:2144], sm[:, :], inc=(s_o, 16))
    elif dump == "arena":
        dump_all(ar[:, :], AR_N)
        b.dma("gpsimd", dbg[0:4, 8192:8208], wgtg[:, :], inc=(s_o, 16))
        b.dma("gpsimd", dbg[0:1, 8208:8256], sgr[:, :], inc=(s_o, 16))
        b.dma("gpsimd", dbg[:, 8256:8272], Z_sb[:, :], inc=(s_o, 16))
        b.dma("gpsimd", dbg[:, 8272:8304], sm[:, :], inc=(s_o, 16))
        b.dma("gpsimd", dbg[:, 8304:10352], S_sb[:, :], inc=(s_o, 16))
        b.dma("gpsimd", dbg[:, 10352:10384], mxn[:, :], inc=(s_o, 16))

    if upto is None:
        for dc in range(DC):
            b.dma("gpsimd", outT[dc * 128:(dc + 1) * 128, :],
                  acc[:, dc * 1024:(dc + 1) * 1024],
                  waits=[(s_V, s_V.n)], inc=(s_o, 16))
    b.emit("gpsimd", lambda e, v=s_o.n: e.wait_ge(sems["s_o"], v))

    with nc.Block() as block:
        @block.sync
        def _(e):
            for f in b.ops["sync"]:
                f(e)

        @block.gpsimd
        def _(e):
            for f in b.ops["gpsimd"]:
                f(e)

        @block.vector
        def _(e):
            for f in b.ops["vector"]:
                f(e)

        @block.scalar
        def _(e):
            for f in b.ops["scalar"]:
                f(e)

        @block.tensor
        def _(e):
            for f in b.ops["tensor"]:
                f(e)

    return nc



def make_in_maps(inputs):
    x = np.asarray(inputs["L0_attn_hidden_states"], dtype=np.float32)
    blob = build_blob(inputs)
    bias = np.asarray(inputs["L0_ffn_bias"], dtype=np.float32)
    gate = np.asarray(inputs["L0_attn_gate"], dtype=np.float32)
    in_maps = []
    for c in range(8):
        seq, half = c // 2, c % 2
        own = x[seq, half * 1024:(half + 1) * 1024, :]
        oth = x[seq, (1 - half) * 1024:(2 - half) * 1024, :]
        xTc = np.ascontiguousarray(np.concatenate([oth, own], axis=0).T)
        cpage = np.zeros((128, 32), dtype=np.float32)
        cpage[0:16, 0] = bias
        cpage[0, 2:18] = gate
        cpage[0, 20] = float(half == 1)
        in_maps.append({"xT": xTc, "blob": blob, "cpage": cpage})
    return in_maps


def kernel(**inputs):
    in_maps = make_in_maps(inputs)
    nc = build_nc()
    res = run_bass_kernel_spmd(nc, in_maps, core_ids=list(range(8)),
                               cleanup_stale_runs=True)
    out = np.empty((4, 2048, 2048), dtype=np.float32)
    for c in range(8):
        seq, half = c // 2, c % 2
        out[seq, half * 1024:(half + 1) * 1024, :] = res.results[c]["outT"].T
    return out
